# version 14
# baseline (speedup 1.0000x reference)
import math
from contextlib import ExitStack
from types import SimpleNamespace as NS

import numpy as np
import ml_dtypes

import concourse.bass as bass
import concourse.mybir as mybir
from concourse.bass_utils import run_bass_kernel_spmd

F32 = mybir.dt.float32
BF16 = mybir.dt.bfloat16
F16 = mybir.dt.float16
I32 = mybir.dt.int32
AF = mybir.ActivationFunctionType
ALU = mybir.AluOpType
BF = ml_dtypes.bfloat16

P = 128
DM = 2048
NT = DM // P
TOK = 1024
HFW = 512
NH = 16
HD = 128
NCH = TOK // HD
FF = 8192
NFT = FF // P
II = 5632
GU = 2 * II
KV = 4096
SCL1 = 1.0 / math.sqrt(HD)
NCORE = 8


def _dperm():
    d = np.zeros(DM, dtype=np.int64)
    for T in range(NT):
        J, p = T // 8, T % 8
        r = np.arange(P)
        d[T * P + r] = 1024 * J + 8 * r + p
    return d


_PERM = _dperm()


def _tile_lhsT(WT):
    IN, OUT = WT.shape
    L = WT.reshape(IN // P, P, OUT // P, P)
    return np.ascontiguousarray(L.transpose(2, 1, 0, 3).reshape(OUT // P, P, IN))


def _slopes(n_heads):
    start = 2.0 ** (-(2.0 ** (-(math.log2(n_heads) - 3))))
    return np.array([start ** (i + 1) for i in range(n_heads)], dtype=np.float64)


def _host_prep(inp):
    pm = _PERM
    f32 = np.float32
    out = {}

    w_qkv = np.asarray(inp["L0_attn_w_qkv"], f32)
    out["wqkv"] = _tile_lhsT(np.ascontiguousarray(w_qkv.T[pm]))
    out["bqkv"] = np.ascontiguousarray(
        np.asarray(inp["L0_attn_b_qkv"], f32).reshape(48, P).T)
    w_out = np.asarray(inp["L0_attn_w_out"], f32)
    out["wout"] = _tile_lhsT(np.ascontiguousarray(w_out.T[:, pm]))
    out["bout"] = np.ascontiguousarray(
        np.asarray(inp["L0_attn_b_out"], f32)[pm].reshape(NT, P).T)

    p1 = np.asarray(inp["L0_ffn_proj1"], f32)
    out["p1"] = _tile_lhsT(np.ascontiguousarray(p1.T[pm]))
    out["p1b"] = np.ascontiguousarray(
        np.asarray(inp["L0_ffn_proj1_bias"], f32).reshape(NFT, P).T)
    p2 = np.asarray(inp["L0_ffn_proj2"], f32)
    A = np.ascontiguousarray(p2.T[:, pm])
    A4 = A.reshape(8, 8, P, NT, P)
    out["p2g"] = np.ascontiguousarray(
        A4.transpose(0, 2, 3, 1, 4).reshape(8, P, NT * 1024))
    out["p2b"] = np.ascontiguousarray(
        np.asarray(inp["L0_ffn_proj2_bias"], f32)[pm].reshape(NT, P).T)

    wq = np.asarray(inp["L1_attn_Wq"], f32)
    out["wq"] = _tile_lhsT(np.ascontiguousarray(wq.T[pm]))
    wo = np.asarray(inp["L1_attn_Wo"], f32)
    out["wo"] = _tile_lhsT(np.ascontiguousarray(wo.T[:, pm]))
    kvk = np.asarray(inp["L1_attn_kv_k"], f32).reshape(KV, NH, HD)
    out["kvkT"] = np.ascontiguousarray(kvk.transpose(1, 2, 0))
    kvv = np.asarray(inp["L1_attn_kv_v"], f32).reshape(KV, NH, HD)
    out["kvv"] = np.ascontiguousarray(kvv.transpose(1, 0, 2))

    gupk = np.asarray(inp["L1_ffn_gate_up_packed"])
    out["gupk"] = np.ascontiguousarray(gupk.T)
    gusc = np.asarray(inp["L1_ffn_gate_up_scales"])
    out["gusc"] = np.ascontiguousarray(
        np.repeat(np.repeat(gusc, 16, axis=0), 2, axis=1))
    dnpk = np.asarray(inp["L1_ffn_down_packed"])
    out["dnpk"] = np.ascontiguousarray(dnpk[pm].T)
    dnsc = np.asarray(inp["L1_ffn_down_scales"])
    out["dnsc"] = np.ascontiguousarray(
        np.repeat(np.repeat(dnsc[:, pm], 16, axis=0), 2, axis=1))

    lam = np.exp(-_slopes(NH))
    pos = np.arange(HD, dtype=np.float64)
    decay_q = lam[:, None] ** (pos + 1.0)
    decay_k = lam[:, None] ** (HD - 1.0 - pos)
    diff = pos[:, None] - pos[None, :]
    dmask = np.where(diff >= 0, lam[:, None, None] ** np.maximum(diff, 0), 0.0)
    lam_c = lam ** HD
    out["dmaskT"] = np.ascontiguousarray(
        dmask.transpose(2, 0, 1).reshape(P, NH * P)).astype(BF)
    out["dqb"] = np.ascontiguousarray(
        np.broadcast_to(decay_q[None, :, :], (P, NH, P))
        .reshape(P, NH * P)).astype(BF)
    out["dk"] = np.ascontiguousarray(decay_k.T.astype(f32))
    pk = np.zeros((P, P), dtype=np.float64)
    for cp in range(8):
        for h in range(NH):
            pk[:, cp * 16 + h] = decay_k[h] * lam_c[h] ** (7 - cp)
    out["pk_odd"] = pk.astype(f32)
    out["pk_even"] = np.zeros((P, P), dtype=f32)
    out["ident"] = np.eye(P, dtype=BF)

    x = np.asarray(inp["L0_attn_x"], f32)

    def fold(sl):
        a = np.ascontiguousarray(sl.T[pm])
        return np.ascontiguousarray(
            a.reshape(NT, P, TOK).transpose(1, 0, 2).reshape(P, NT * TOK))
    xs, xps = [], []
    for c in range(NCORE):
        b, half = c // 2, c % 2
        xs.append(fold(x[b, half * TOK:(half + 1) * TOK]))
        xps.append(fold(x[b, 0:TOK]) if half == 1 else xs[-1])
    out["x_cores"] = xs
    out["xpre_cores"] = xps
    return out


def _unfold_out(res_c):
    a = np.asarray(res_c, np.float32).reshape(P, NT, TOK).transpose(1, 0, 2)
    a = a.reshape(DM, TOK)
    nat = np.empty((TOK, DM), np.float32)
    nat[:, _PERM] = a.T
    return nat


class _Sem:
    __slots__ = ("h", "n")


class Prog:
    ENGINES = ("sync", "gpsimd", "tensor", "vector", "scalar")

    def __init__(self, nc, ctx):
        self.nc = nc
        self.ctx = ctx
        self.ops = {e: [] for e in self.ENGINES}

    def sem(self, name):
        s = _Sem()
        s.h = self.ctx.enter_context(self.nc.semaphore(name))
        s.n = 0
        return s

    def op(self, eng, fn, waits=(), inc=None, dma=False):
        k = 16 if dma else 1
        w = tuple(x for x in waits if x is not None)
        self.ops[eng].append((fn, w, inc, k))
        if inc is not None:
            inc.n += k
            return (inc, inc.n)
        return None

    def emit(self, block):
        for e in self.ENGINES:
            items = self.ops[e]
            if not items:
                continue

            def body(eng, items=items):
                seen = {}
                for fn, waits, inc, k in items:
                    for (s, v) in waits:
                        if seen.get(id(s), -1) >= v:
                            continue
                        eng.wait_ge(s.h, v)
                        seen[id(s)] = v
                    inst = fn(eng)
                    if inc is not None:
                        inst.then_inc(inc.h, k)
            getattr(block, e)(body)


def _mk_sems(p):
    return {k: p.sem(k) for k in
            ("t", "v", "s", "d", "g", "m", "g0", "g1", "d0", "d1", "g2", "g3",
             "dql0", "dql1", "dqs0", "dqs1", "d2", "g4")}


def _stage_end(sm):
    return [(s, s.n) for s in sm.values()]


def _emit_ln(p, sm, x_sb, xn_sb, sq_sb, st_sb, ab_sb, cst, ones_f, ones1_f,
             ps_r, ps_w0, ps_w1, eps_col, x_ready, ps_free=(), rms=False,
             ones_b=None, x_ready_t=None, x_bf16=False):
    t, v, s = sm["t"], sm["v"], sm["s"]
    xr = list(x_ready)
    pf = list(ps_free)
    sqb = sq_sb.bitcast(BF16)
    sq_tok = [None, None]
    mm_tok = None
    for T in range(NT):
        sl = T % 2
        xrT = xr + ([x_ready_t[T]] if x_ready_t else [])
        tok_sq = p.op("scalar", lambda e, T=T, sl=sl: e.activation(
            sqb[:, sl * 2048: sl * 2048 + 1024],
            x_sb[:, T * 1024:(T + 1) * 1024],
            AF.Square), waits=xrT + [sq_tok[sl]], inc=s)
        if not rms:
            ones_sum = ones_b if x_bf16 else ones_f
            for hf in range(2):
                p.op("tensor", lambda e, T=T, hf=hf, ones_sum=ones_sum:
                     e.matmul(
                    ps_r[0:1, hf * HFW:(hf + 1) * HFW], ones_sum[:, 0:1],
                    x_sb[:, T * 1024 + hf * HFW: T * 1024 + (hf + 1) * HFW],
                    start=(T == 0), stop=(T == NT - 1)),
                    waits=xrT + (pf if T == 0 else []))
        for hf in range(2):
            mm_tok = p.op("tensor", lambda e, T=T, hf=hf, sl=sl: e.matmul(
                ps_w1[0:1, hf * HFW:(hf + 1) * HFW], ones_b[:, 0:1],
                sqb[:, sl * 2048 + hf * HFW: sl * 2048 + (hf + 1) * HFW],
                start=(T == 0), stop=(T == NT - 1)),
                waits=[tok_sq] + (pf if T == 0 else []), inc=t)
        sq_tok[sl] = mm_tok
    inv = 1.0 / DM
    if rms:
        var_tok = p.op("vector", lambda e: e.tensor_scalar(
            st_sb[0:1, 3072:4096], ps_w1[0:1, 0:1024], inv, None, ALU.mult),
            waits=[mm_tok], inc=v)
    else:
        tok_m = p.op("vector", lambda e: e.tensor_scalar(
            st_sb[0:1, 0:1024], ps_r[0:1, 0:1024], inv, None, ALU.mult),
            waits=[mm_tok], inc=v)
        tok_q = p.op("vector", lambda e: e.tensor_scalar(
            st_sb[0:1, 3072:4096], ps_w1[0:1, 0:1024], inv, None, ALU.mult),
            waits=[mm_tok], inc=v)
        tok_mm = p.op("vector", lambda e: e.tensor_tensor(
            st_sb[0:1, 2048:3072], st_sb[0:1, 0:1024], st_sb[0:1, 0:1024],
            ALU.mult), waits=[tok_m], inc=v)
        var_tok = p.op("vector", lambda e: e.tensor_tensor(
            st_sb[0:1, 3072:4096], st_sb[0:1, 3072:4096], st_sb[0:1, 2048:3072],
            ALU.subtract), waits=[tok_q, tok_mm], inc=v)
    tok_ln = p.op("scalar", lambda e: e.activation(
        st_sb[0:1, 2048:3072], st_sb[0:1, 3072:4096], AF.Ln,
        bias=cst[0:1, eps_col:eps_col + 1]), waits=[var_tok] + xr, inc=s)
    p.ln_ps_done = [var_tok]
    tok_rs = p.op("scalar", lambda e: e.activation(
        st_sb[0:1, 1024:2048], st_sb[0:1, 2048:3072], AF.Exp, scale=-0.5),
        waits=[tok_ln], inc=s)
    bc_tok = None
    for hf in range(2):
        bc_tok = p.op("tensor", lambda e, hf=hf: e.matmul(
            ps_w0[:, hf * HFW:(hf + 1) * HFW], ones1_f[0:1, :],
            st_sb[0:1, 1024 + hf * HFW:1024 + (hf + 1) * HFW],
            start=True, stop=True), waits=[tok_rs] + xr + pf, inc=t)
    a_tok = p.op("scalar", lambda e: e.copy(ab_sb[:, 0:1024], ps_w0[:, 0:1024]),
                 waits=[bc_tok], inc=s)
    if rms:
        return a_tok, a_tok, None
    tok_b = p.op("vector", lambda e: e.scalar_tensor_tensor(
        st_sb[0:1, 0:1024], st_sb[0:1, 0:1024], -1.0, st_sb[0:1, 1024:2048],
        ALU.mult, ALU.mult), waits=[tok_rs, tok_m], inc=v)
    bc2_tok = None
    for hf in range(2):
        bc2_tok = p.op("tensor", lambda e, hf=hf: e.matmul(
            ps_w0[:, hf * HFW:(hf + 1) * HFW], ones1_f[0:1, :],
            st_sb[0:1, hf * HFW:(hf + 1) * HFW],
            start=True, stop=True), waits=[tok_b, a_tok], inc=t)
    b_tok = p.op("scalar", lambda e: e.copy(ab_sb[:, 1024:2048], ps_w0[:, 0:1024]),
                 waits=[bc2_tok], inc=s)
    p.ln_ps_done = [var_tok, b_tok]
    done = None
    done_t = []
    for T in range(NT):
        sl = T % 2
        t1 = p.op("vector", lambda e, T=T, sl=sl: e.tensor_tensor(
            sq_sb[:, sl * 1024:(sl + 1) * 1024], x_sb[:, T * 1024:(T + 1) * 1024],
            ab_sb[:, 0:1024], ALU.mult), waits=[a_tok, b_tok], inc=v)
        done = p.op("vector", lambda e, T=T, sl=sl: e.tensor_tensor(
            xn_sb[:, T * 1024:(T + 1) * 1024], sq_sb[:, sl * 1024:(sl + 1) * 1024],
            ab_sb[:, 1024:2048], ALU.add), waits=[t1], inc=v)
        done_t.append(done)
    return done, a_tok, done_t


def _alloc(nc, e_):
    A = NS()
    A.A1 = e_(nc.sbuf_tensor("A1", [P, NT * TOK], F32))
    A.A2 = e_(nc.sbuf_tensor("A2", [P, NT * TOK], BF16))
    A.A3 = e_(nc.sbuf_tensor("A3", [P, NT * TOK], BF16))
    A.A4 = e_(nc.sbuf_tensor("A4", [P, NT * TOK], BF16))
    A.A5 = e_(nc.sbuf_tensor("A5", [P, NT * TOK], BF16))
    A.cst = e_(nc.sbuf_tensor("cstS", [P, 4], F32))
    A.ones_f = e_(nc.sbuf_tensor("onesS", [P, 1], F32))
    A.ones1_f = e_(nc.sbuf_tensor("ones1S", [1, P], F32))
    A.ones_b = e_(nc.sbuf_tensor("onebS", [P, 1], BF16))
    A.msk = e_(nc.sbuf_tensor("mskS", [P, NH * P], BF16))
    A.dqb = e_(nc.sbuf_tensor("dqbS", [P, NH * P], BF16))
    A.dk = e_(nc.sbuf_tensor("dkS", [P, NH], F32))
    A.pk = e_(nc.sbuf_tensor("pkS", [P, P], F32))
    A.idn = e_(nc.sbuf_tensor("idnS", [P, P], BF16))
    A.bq = e_(nc.sbuf_tensor("bqS", [P, 48], F32))
    A.vph = e_(nc.sbuf_tensor("vphS", [P, 2 * TOK], BF16))
    A.bo = e_(nc.sbuf_tensor("boS", [P, NT], F32))
    A.p1b = e_(nc.sbuf_tensor("p1bS", [P, NFT], F32))
    A.p2b = e_(nc.sbuf_tensor("p2bS", [P, NT], F32))
    A.ps_w0 = e_(nc.psum_tensor("ps_w0", [P, 1024], F32))
    A.ps_w1 = e_(nc.psum_tensor("ps_w1", [P, 1024], F32))
    A.ps_r = e_(nc.psum_tensor("ps_r", [P, 1024], F32))
    A.ps_tr = e_(nc.psum_tensor("ps_tr", [P, 1024], BF16))
    A.ps_tr2 = e_(nc.psum_tensor("ps_tr2", [P, 1024], BF16))
    return A


def _emit_consts(p, sm, A):
    m = sm["m"]
    p.op("gpsimd", lambda e: e.memset(A.cst[:, 0:1], -16.0), inc=m)
    p.op("gpsimd", lambda e: e.memset(A.cst[:, 1:2], 1e-5), inc=m)
    p.op("gpsimd", lambda e: e.memset(A.cst[:, 2:3], 1e-6), inc=m)
    p.op("gpsimd", lambda e: e.memset(A.ones_f[:], 1.0), inc=m)
    p.op("gpsimd", lambda e: e.memset(A.ones1_f[:], 1.0), inc=m)
    tok = p.op("gpsimd", lambda e: e.memset(A.ones_b[:], 1.0), inc=m)
    return tok


def _emit_s1(p, sm, A, D, entry, mtok):
    t, v, s, d, g = (sm[k] for k in ("t", "v", "s", "d", "g"))
    g01 = (sm["g0"], sm["g1"])
    d01 = (sm["d0"], sm["d1"])
    lam_c_vals = (np.exp(-_slopes(NH)) ** HD).astype(np.float64)

    xf = A.A1[:]
    xn = A.A2[:]
    xnp = A.A3[:]
    wsl = A.A4[:, 0:12288]
    attn = A.A4[:, 12288:14336]
    ktc = A.A4[:, 14336:14592]
    vtc = A.A4[:, 14592:14848]
    kpp = A.A4[:, 14848:15104]
    vpp = A.A4[:, 15104:15360]
    qp = A.A4[:, 15360:15616]
    pt = A.A4[:, 15616:15872]
    S = A.A4[:, 15872:16128].bitcast(F32)
    Sb = A.A4[:, 16128:16256]
    st = A.A4[0:1, 0:8192].bitcast(F32)
    sq = A.A5[:, 0:4096].bitcast(F32)
    ab = A.A5[:, 4096:8192].bitcast(F32)
    qh = A.A5[:, 8192:10240]
    kh = A.A5[:, 10240:12288]
    vh = A.A5[:, 12288:14336]
    kph = A.A5[:, 14336:16384]
    vph = A.vph[:]
    msk, dqb, dk, pk, idn, bq = A.msk, A.dqb, A.dk, A.pk, A.idn, A.bq
    ps_w0, ps_w1, ps_tr, ps_tr2, ps_r = (A.ps_w0, A.ps_w1, A.ps_tr, A.ps_tr2,
                                         A.ps_r)

    xpre_bf = A.A1[:, 8192:16384].bitcast(BF16)
    x_bf = A.A1[:, 0:8192].bitcast(BF16)
    for dst, srcd in ((msk[:], D.msk), (dqb[:], D.dqb), (dk[:], D.dk),
                      (pk[:], D.pk), (idn[:], D.idn), (bq[:], D.bq)):
        p.op("sync", lambda e, dst=dst, srcd=srcd:
             e.dma_start(out=dst, in_=srcd), waits=entry, inc=d, dma=True)
    tok_msk = tok_dqb = tok_dk = tok_pk = tok_id = tok_bq = (
        sm["d"], sm["d"].n)
    sq2 = A.A5[:, 8192:12288].bitcast(F32)
    st2 = A.A4[0:1, 8192:16384].bitcast(F32)
    ab2 = A.A5[:, 12288:16384].bitcast(F32)
    xp_sems = [sm["g4"], sm["g"], sm["g0"], sm["g1"]]
    xp_q = []
    for qq in range(4):
        xp_q.append(p.op("gpsimd", lambda e, qq=qq: e.dma_start(
            out=xpre_bf[:, qq * 4096:(qq + 1) * 4096],
            in_=D.xpre[:, qq * 4096:(qq + 1) * 4096]),
            waits=entry if qq == 0 else [], inc=xp_sems[qq], dma=True))
    xb_sems = [sm["g2"], sm["g3"]]
    xb_h = []
    for qq in range(2):
        xb_h.append(p.op("gpsimd", lambda e, qq=qq: e.dma_start(
            out=x_bf[:, qq * 8192:(qq + 1) * 8192],
            in_=D.x[:, qq * 8192:(qq + 1) * 8192]),
            inc=xb_sems[qq], dma=True))
    xb_q = [xb_h[0], xb_h[0], xb_h[1], xb_h[1]]

    lnp_done, _, lnp_t = _emit_ln(p, sm, xpre_bf, xnp, sq, st, ab, A.cst,
                           A.ones_f, A.ones1_f, ps_r[:], ps_w0[:], ps_w1[:],
                           eps_col=1, x_ready=[mtok] + entry,
                           x_ready_t=[xp_q[T // 4] for T in range(NT)],
                           ones_b=A.ones_b, x_bf16=True)
    ln_done, _, ln_t = _emit_ln(p, sm, x_bf, xn, sq2, st2, ab2, A.cst,
                          A.ones_f, A.ones1_f, ps_r[:], ps_w0[:], ps_w1[:],
                          eps_col=1, x_ready=[mtok],
                          x_ready_t=[xb_q[T // 4] for T in range(NT)],
                          ps_free=list(p.ln_ps_done), ones_b=A.ones_b,
                          x_bf16=True)

    a1_free = [ln_done, lnp_done, (s, s.n), (t, t.n)]

    wload = [None, None]
    wuse = [None, None]
    head_end = [None, None]
    ev = {k: [None, None] for k in ("q", "k", "v", "kp", "vp")}
    pw_free = [lnp_done, lnp_done]
    tr_free = [None, None]
    sc_free = None
    at_free = None
    pr_free = ln_done
    ktc_r = [None, None]
    vtc_r = [None, None]
    kpp_r = [None, None]
    vpp_r = [None, None]
    qp_r = [None, None]
    pt_r = [None, None]
    Sb_r = None
    tS = None
    tSb = None
    attn_st = [None, None]
    w0_extra = [ln_done, (s, s.n), (t, t.n)] + entry

    for h in range(NH):
        sl = h % 2
        base = sl * 3 * DM
        lt = None
        for j, ot in enumerate((h, 16 + h, 32 + h)):
            lt = p.op("gpsimd", lambda e, j=j, ot=ot, base=base: e.dma_start(
                out=wsl[:, base + j * DM: base + (j + 1) * DM], in_=D.wqkv[ot]),
                waits=([wuse[sl]] if wuse[sl] else list(w0_extra))
                      if j == 0 else [],
                inc=g01[sl], dma=True)
        wload[sl] = lt

        ops = [("kp", kph, 1, xnp, 16 + h, lnp_t),
               ("vp", vph, 2, xnp, 32 + h, lnp_t),
               ("q", qh, 0, xn, h, ln_t),
               ("k", kh, 1, xn, 16 + h, ln_t),
               ("v", vh, 2, xn, 32 + h, ln_t)]
        for i, (nm, dstb, wj, srcb, bcol, stok_t) in enumerate(ops):
            ps = (ps_w0, ps_w1)[i % 2]
            pw = [wload[sl], pw_free[i % 2]]
            mm = None
            for hf in range(2):
                for kt in range(NT):
                    mm = p.op("tensor", lambda e, ps=ps, hf=hf, kt=kt,
                              base=base, wj=wj, srcb=srcb: e.matmul(
                        ps[:, hf * HFW:(hf + 1) * HFW],
                        wsl[:, base + wj * DM + kt * P:
                             base + wj * DM + (kt + 1) * P],
                        srcb[:, kt * TOK + hf * HFW:
                             kt * TOK + (hf + 1) * HFW],
                        start=(kt == 0), stop=(kt == NT - 1)),
                        waits=pw + [stok_t[kt]],
                        inc=(t if (hf == 1 and kt == NT - 1)
                             else None))
            wuse[sl] = mm
            evw = [mm, ev[nm][sl], tok_bq]
            if head_end[sl] is not None:
                evw += list(head_end[sl])
            etok = p.op("scalar", lambda e, ps=ps, dstb=dstb, sl=sl,
                        bcol=bcol: e.activation(
                dstb[:, sl * TOK:(sl + 1) * TOK], ps[:, 0:1024], AF.Identity,
                bias=bq[:, bcol:bcol + 1]), waits=evw, inc=s)
            ev[nm][sl] = etok
            pw_free[i % 2] = etok

        sS = None
        for cp in range(8):
            csl = cp % 2
            t1 = p.op("tensor", lambda e, cp=cp, sl=sl: e.transpose(
                ps_tr[:, 0:P],
                kph[:, sl * TOK + cp * P: sl * TOK + (cp + 1) * P], idn[:]),
                waits=[ev["kp"][sl], tr_free[0], tok_id], inc=t)
            e1 = p.op("scalar", lambda e, cp=cp, csl=csl, h=h: e.activation(
                kpp[:, csl * P:(csl + 1) * P], ps_tr[:, 0:P], AF.Copy,
                scale=pk[:, cp * 16 + h:cp * 16 + h + 1]),
                waits=[t1, tok_pk, kpp_r[csl]], inc=s)
            tr_free[0] = e1
            t2 = p.op("tensor", lambda e, cp=cp, sl=sl: e.transpose(
                ps_tr2[:, 0:P],
                vph[:, sl * TOK + cp * P: sl * TOK + (cp + 1) * P], idn[:]),
                waits=[ev["vp"][sl], tr_free[1], tok_id], inc=t)
            e2 = p.op("scalar", lambda e, csl=csl: e.copy(
                vpp[:, csl * P:(csl + 1) * P], ps_tr2[:, 0:P]),
                waits=[t2, vpp_r[csl]], inc=s)
            tr_free[1] = e2
            sS = p.op("tensor", lambda e, cp=cp, csl=csl: e.matmul(
                ps_r[:, 0:P], kpp[:, csl * P:(csl + 1) * P],
                vpp[:, csl * P:(csl + 1) * P],
                start=(cp == 0), stop=(cp == 7)),
                waits=[e1, e2] + ([pr_free] if cp == 0 else []), inc=t)
            kpp_r[csl] = sS
            vpp_r[csl] = sS
        tS = p.op("vector", lambda e: e.tensor_copy(S[:], ps_r[:, 0:P]),
                  waits=[sS], inc=v)
        tSb = p.op("vector", lambda e: e.tensor_copy(Sb[:], S[:]),
                   waits=[tS, Sb_r], inc=v)
        pr_free = tS

        lamch = float(lam_c_vals[h])
        for c in range(NCH):
            csl = c % 2
            mm_sc = p.op("tensor", lambda e, c=c, sl=sl: e.matmul(
                ps_w0[:, 0:P],
                kh[:, sl * TOK + c * P: sl * TOK + (c + 1) * P],
                qh[:, sl * TOK + c * P: sl * TOK + (c + 1) * P],
                start=True, stop=True),
                waits=[ev["k"][sl], ev["q"][sl], sc_free] +
                      ([pw_free[0]] if c == 0 else []), inc=t)
            tok_pt = p.op("vector", lambda e, csl=csl, h=h: e.tensor_tensor(
                pt[:, csl * P:(csl + 1) * P], ps_w0[:, 0:P],
                msk[:, h * P:(h + 1) * P], ALU.mult),
                waits=[mm_sc, tok_msk, pt_r[csl]], inc=v)
            sc_free = tok_pt
            t1 = p.op("tensor", lambda e, c=c, sl=sl: e.transpose(
                ps_tr[:, 0:P],
                kh[:, sl * TOK + c * P: sl * TOK + (c + 1) * P], idn[:]),
                waits=[ev["k"][sl], tr_free[0]], inc=t)
            e1 = p.op("scalar", lambda e, csl=csl, h=h: e.activation(
                ktc[:, csl * P:(csl + 1) * P], ps_tr[:, 0:P], AF.Copy,
                scale=dk[:, h:h + 1]), waits=[t1, tok_dk, ktc_r[csl]], inc=s)
            tr_free[0] = e1
            t2 = p.op("tensor", lambda e, c=c, sl=sl: e.transpose(
                ps_tr2[:, 0:P],
                vh[:, sl * TOK + c * P: sl * TOK + (c + 1) * P], idn[:]),
                waits=[ev["v"][sl], tr_free[1]], inc=t)
            e2 = p.op("scalar", lambda e, csl=csl: e.copy(
                vtc[:, csl * P:(csl + 1) * P], ps_tr2[:, 0:P]),
                waits=[t2, vtc_r[csl]], inc=s)
            tr_free[1] = e2
            tok_qp = p.op("vector", lambda e, c=c, sl=sl, csl=csl, h=h:
                          e.tensor_tensor(
                qp[:, csl * P:(csl + 1) * P],
                qh[:, sl * TOK + c * P: sl * TOK + (c + 1) * P],
                dqb[:, h * P:(h + 1) * P], ALU.mult),
                waits=[ev["q"][sl], tok_dqb, qp_r[csl]], inc=v)
            p.op("tensor", lambda e, csl=csl: e.matmul(
                ps_w1[:, 0:P], vtc[:, csl * P:(csl + 1) * P],
                pt[:, csl * P:(csl + 1) * P], start=True, stop=False),
                waits=[tok_pt, e2, at_free] +
                      ([pw_free[1]] if c == 0 else []))
            mm_at = p.op("tensor", lambda e, csl=csl: e.matmul(
                ps_w1[:, 0:P], Sb[:], qp[:, csl * P:(csl + 1) * P],
                start=False, stop=True), waits=[tok_qp, tSb], inc=t)
            pt_r[csl] = mm_at
            qp_r[csl] = mm_at
            Sb_r = mm_at
            at_free = p.op("scalar", lambda e, sl=sl, c=c: e.copy(
                attn[:, sl * TOK + c * P: sl * TOK + (c + 1) * P],
                ps_w1[:, 0:P]), waits=[mm_at, attn_st[sl]], inc=s)
            mm_s = p.op("tensor", lambda e, csl=csl: e.matmul(
                ps_r[:, 0:P], ktc[:, csl * P:(csl + 1) * P],
                vtc[:, csl * P:(csl + 1) * P], start=True, stop=True),
                waits=[e1, e2, pr_free], inc=t)
            ktc_r[csl] = mm_s
            vtc_r[csl] = mm_s
            tS = p.op("vector", lambda e, lamch=lamch: e.scalar_tensor_tensor(
                S[:], S[:], lamch, ps_r[:, 0:P], ALU.mult, ALU.add),
                waits=[mm_s], inc=v)
            pr_free = tS
            if c < NCH - 1:
                tSb = p.op("vector", lambda e: e.tensor_copy(Sb[:], S[:]),
                           waits=[tS, Sb_r], inc=v)
        head_end[sl] = [(t, t.n), (v, v.n)]
        pw_free = [sc_free, at_free]
        attn_st[sl] = p.op("sync", lambda e, h=h, sl=sl: e.dma_start(
            out=D.attn[:, h * TOK:(h + 1) * TOK],
            in_=attn[:, sl * TOK:(sl + 1) * TOK]),
            waits=[at_free], inc=d01[sl], dma=True)

    tok_xnst = p.op("sync", lambda e: e.dma_start(out=D.xn, in_=xn),
                    waits=[ln_done], inc=d, dma=True)
    return {"attn_st": [attn_st[0], attn_st[1]], "xn_st": tok_xnst,
            "a1_free": a1_free}


def _emit_s2(p, sm, A, D, entry, mtok, hook=None, a1_free=None):
    t, v, s, d, g = (sm[k] for k in ("t", "v", "s", "d", "g"))
    g01 = (sm["g0"], sm["g1"])

    x = A.A1[:]
    xn = A.A2[:]
    big = A.A3[:]
    h1g = A.A4[:, 0:8192]
    wsl = A.A4[:, 8192:12288]
    st = A.A4[0:1, 0:8192].bitcast(F32)
    sq = A.A5[:, 0:4096].bitcast(F32)
    ab = A.A5[:, 4096:8192].bitcast(F32)
    bo, p1b, p2b = A.bo, A.p1b, A.p2b
    ps_w0, ps_w1, ps_r = A.ps_w0, A.ps_w1, A.ps_r

    tok_xl = p.op("sync", lambda e: e.dma_start(out=x, in_=D.x),
                  waits=(a1_free if a1_free else entry), inc=sm["d2"],
                  dma=True)
    p.op("sync", lambda e: e.dma_start(out=xn, in_=D.xn), waits=entry,
         inc=d, dma=True)
    p.op("sync", lambda e: e.dma_start(out=big, in_=D.attn), inc=d, dma=True)
    cl = None
    for dst, srcd in ((bo[:], D.bo), (p1b[:], D.p1b), (p2b[:], D.p2b)):
        cl = p.op("sync", lambda e, dst=dst, srcd=srcd:
                  e.dma_start(out=dst, in_=srcd), inc=d, dma=True)
    tok_xn = tok_at = cl
    tok_x = tok_xl

    wl = [None, None]
    wu = [None, None]
    pw = [None, None]
    xadd = None
    for ot in range(NT):
        sl = ot % 2
        wl[sl] = p.op("gpsimd", lambda e, ot=ot, sl=sl: e.dma_start(
            out=wsl[:, sl * DM:(sl + 1) * DM], in_=D.wout[ot]),
            waits=[wu[sl]] if wu[sl] else entry, inc=g01[sl], dma=True)
        ps = (ps_w0, ps_w1)[sl]
        mm = None
        for hf in range(2):
            for kt in range(NT):
                mm = p.op("tensor", lambda e, ps=ps, sl=sl, hf=hf, kt=kt:
                          e.matmul(
                    ps[:, hf * HFW:(hf + 1) * HFW],
                    wsl[:, sl * DM + kt * P: sl * DM + (kt + 1) * P],
                    big[:, kt * TOK + hf * HFW: kt * TOK + (hf + 1) * HFW],
                    start=(kt == 0), stop=(kt == NT - 1)),
                    waits=[wl[sl], tok_at, pw[sl]] + (entry if ot < 2 else []),
                    inc=(t if (hf == 1 and kt == NT - 1) else None))
        wu[sl] = mm
        for hf in range(2):
            xadd = p.op("vector", lambda e, ps=ps, ot=ot, hf=hf:
                        e.scalar_tensor_tensor(
                x[:, ot * TOK + hf * HFW: ot * TOK + (hf + 1) * HFW],
                ps[:, hf * HFW:(hf + 1) * HFW], bo[:, ot:ot + 1],
                x[:, ot * TOK + hf * HFW: ot * TOK + (hf + 1) * HFW],
                ALU.add, ALU.add), waits=[mm, tok_x, cl] + entry, inc=v)
        pw[sl] = xadd
    outproj_done = xadd

    p1l = [None, None]
    p1u = [None, None]
    gelu_tok = [None] * 8
    p2sl = A.A4[:, 12288:14336]
    p2l = [None, None]
    p2u = [None, None]
    ps_free = {0: outproj_done, 1: outproj_done}
    ffn_add = outproj_done
    for gb in range(8):
        for kt in range(8):
            sl = kt % 2
            ht = gb * 8 + kt
            p1l[sl] = p.op("gpsimd", lambda e, ht=ht, sl=sl: e.dma_start(
                out=wsl[:, sl * DM:(sl + 1) * DM], in_=D.p1[ht]),
                waits=[p1u[sl] if p1u[sl] else wu[sl]], inc=g01[sl],
                dma=True)
            mm = None
            for hf in range(2):
                for ik in range(NT):
                    mm = p.op("tensor", lambda e, sl=sl, hf=hf, ik=ik:
                              e.matmul(
                        ps_w0[:, hf * HFW:(hf + 1) * HFW],
                        wsl[:, sl * DM + ik * P: sl * DM + (ik + 1) * P],
                        xn[:, ik * TOK + hf * HFW: ik * TOK + (hf + 1) * HFW],
                        start=(ik == 0), stop=(ik == NT - 1)),
                        waits=[p1l[sl], tok_xn, outproj_done,
                               gelu_tok[(kt - 1) % 8]],
                        inc=(t if (hf == 1 and ik == NT - 1) else None))
            p1u[sl] = mm
            gelu_tok[kt] = p.op("scalar", lambda e, kt=kt, ht=ht:
                                e.activation(
                h1g[:, kt * TOK:(kt + 1) * TOK], ps_w0[:, 0:1024], AF.Gelu,
                bias=p1b[:, ht:ht + 1]), waits=[mm, cl], inc=s)
        for ot in range(NT):
            psel = ot % 2
            sl2 = ot % 2
            p2l[sl2] = p.op("gpsimd", lambda e, gb=gb, ot=ot, sl2=sl2:
                            e.dma_start(
                out=p2sl[:, sl2 * 1024:(sl2 + 1) * 1024],
                in_=D.p2g[gb, :, ot * 1024:(ot + 1) * 1024]),
                waits=([p2u[sl2]] if p2u[sl2] else
                       ([(t, t.n)] + entry if gb == 0 and ot < 2 else [])),
                inc=sm["g2"] if sl2 == 0 else sm["g3"], dma=True)
            ps = (ps_w1, ps_r)[psel]
            mm = None
            for hf in range(2):
                for kt in range(8):
                    mm = p.op("tensor", lambda e, ps=ps, sl2=sl2, hf=hf, kt=kt:
                              e.matmul(
                        ps[:, hf * HFW:(hf + 1) * HFW],
                        p2sl[:, sl2 * 1024 + kt * P: sl2 * 1024 + (kt + 1) * P],
                        h1g[:, kt * TOK + hf * HFW: kt * TOK + (hf + 1) * HFW],
                        start=(kt == 0), stop=(kt == 7)),
                        waits=[p2l[sl2], gelu_tok[kt], ps_free[psel]],
                        inc=(t if (hf == 1 and kt == 7) else None))
            p2u[sl2] = mm
            for hf in range(2):
                if gb == 0:
                    ffn_add = p.op("vector", lambda e, ps=ps, ot=ot, hf=hf:
                                   e.scalar_tensor_tensor(
                        x[:, ot * TOK + hf * HFW: ot * TOK + (hf + 1) * HFW],
                        ps[:, hf * HFW:(hf + 1) * HFW], p2b[:, ot:ot + 1],
                        x[:, ot * TOK + hf * HFW: ot * TOK + (hf + 1) * HFW],
                        ALU.add, ALU.add), waits=[mm, outproj_done, cl],
                        inc=v)
                else:
                    ffn_add = p.op("vector", lambda e, ps=ps, ot=ot, hf=hf:
                                   e.tensor_tensor(
                        x[:, ot * TOK + hf * HFW: ot * TOK + (hf + 1) * HFW],
                        x[:, ot * TOK + hf * HFW: ot * TOK + (hf + 1) * HFW],
                        ps[:, hf * HFW:(hf + 1) * HFW], ALU.add),
                        waits=[mm], inc=v)
            ps_free[psel] = ffn_add
        if hook is not None:
            hook(gb)

    st1 = p.op("sync", lambda e: e.dma_start(out=D.x1, in_=x),
               waits=[ffn_add], inc=d, dma=True)
    ln1_done, _, ln1_t = _emit_ln(p, sm, x, xn, sq, st, ab, A.cst, A.ones_f,
                           A.ones1_f, ps_r[:], ps_w0[:], ps_w1[:], eps_col=1,
                           x_ready=[ffn_add, mtok],
                           ps_free=[(t, t.n)], ones_b=A.ones_b)
    st2 = p.op("sync", lambda e: e.dma_start(out=D.xn1, in_=xn),
               waits=[ln1_done], inc=d, dma=True)
    a1f2 = [ln1_done, (s, s.n), (t, t.n), st2]
    return {"x1_st": st1, "xn1_st": st2, "ln1_done": ln1_done,
            "ln1_t": ln1_t, "a1f2": a1f2}



class _DeqEarly:

    LOOKAHEAD = 2

    def __init__(self, p, sm, A, D, entry):
        self.p, self.sm, self.A, self.D = p, sm, A, D
        self.entry = list(entry)
        self.R = None
        self.guard = list(entry)
        self.ld_region = {}
        self.units = []
        NB = [1024] * 5 + [512]
        for Jp in range(6):
            W = NB[Jp]
            for sect in range(2):
                sec = Jp * 2 + sect
                ob = Jp * 1024 if sect == 0 else II + Jp * 1024
                for J in range(2):
                    for pair in range(4):
                        self.units.append(("gu", sec, J, pair, ob, W))
        for Jp in range(6):
            for pair in range(4):
                for jh in range(2):
                    self.units.append(("dn", Jp, jh, pair, 0, 1024))
        self.idx = 0
        self.nload = 0
        self.t6_chain = [None, None]
        self.st_chain = [None, None]
        self.store_tok = {}
        self.ld_tok = {}

    def set_region(self, which, guard):
        R = NS()
        if which == "a1":
            a1 = self.A.A1
            R.stg = [a1[:, 6144:8192].bitcast(BF16),
                     a1[:, 8192:10240].bitcast(BF16)]
            R.pk2 = a1[:, 10240:12288].bitcast(I32)
            R.sc2 = a1[:, 12288:14336].bitcast(F16)
            R.dtmp = a1[:, 14336:16384].bitcast(BF16)
            R.dsg = a1[:, 14336:16384].bitcast(I32)
        else:
            a3 = self.A.A3
            R.stg = [a3[:, 0:2048], a3[:, 2048:4096]]
            R.pk2 = a3[:, 4096:8192].bitcast(I32)
            R.sc2 = a3[:, 8192:12288].bitcast(F16)
            R.dtmp = a3[:, 12288:14336]
            R.dsg = a3[:, 14336:16384].bitcast(I32)
        self.R = R
        self.guard = list(guard)

    def take(self, k):
        for _ in range(k):
            if self.idx >= len(self.units):
                return
            while self.nload < min(self.idx + self.LOOKAHEAD,
                                   len(self.units)):
                self._loads(self.nload)
                self.nload += 1
            self._compute(self.units[self.idx])
            self.idx += 1

    def finish(self):
        self.take(len(self.units))

    def drain(self):
        while self.idx < self.nload:
            self._compute(self.units[self.idx])
            self.idx += 1

    def _views(self, u, q, Rg):
        kind, sec, J, pair, ob, W = u
        if kind == "gu":
            R = P
            pk = Rg.pk2[:, q * 1024: q * 1024 + W]
            sc = Rg.sc2[:, q * 2048: q * 2048 + 2 * W]
            srcp = self.D.gupk[J * P:(J + 1) * P, ob: ob + W]
            srcs = self.D.gusc[J * P:(J + 1) * P, 2 * ob: 2 * ob + 2 * W]
            dst = self.D.gw[sec][:, (J * 4 + pair) * 2048:
                                 (J * 4 + pair) * 2048 + 2 * W]
        else:
            Jp, jh = sec, J
            R = 128 if Jp < 5 else 64
            pk = Rg.pk2[0:R, q * 1024:(q + 1) * 1024]
            sc = Rg.sc2[0:R, q * 2048:(q + 1) * 2048]
            srcp = self.D.dnpk[Jp * P: Jp * P + R, jh * 1024:(jh + 1) * 1024]
            srcs = self.D.dnsc[Jp * P: Jp * P + R, jh * 2048:(jh + 1) * 2048]
            dst = self.D.dw[Jp][0:R, pair * 4096 + jh * 2048:
                                pair * 4096 + jh * 2048 + 2048]
        return pk, sc, srcp, srcs, dst, R

    def _loads(self, uidx):
        p, sm = self.p, self.sm
        u = self.units[uidx]
        q = uidx % 2
        dql = sm["dql0"] if q == 0 else sm["dql1"]
        self.ld_region[uidx] = self.R
        pk, sc, srcp, srcs, dst, R = self._views(u, q, self.R)
        lw = [self.t6_chain[q]] + self.guard
        p.op("sync", lambda e, pk=pk, srcp=srcp: e.dma_start(
            out=pk, in_=srcp), waits=lw, inc=dql, dma=True)
        l2 = p.op("sync", lambda e, sc=sc, srcs=srcs: e.dma_start(
            out=sc, in_=srcs), inc=dql, dma=True)
        self.ld_tok[uidx] = l2

    def _compute(self, u):
        p, sm = self.p, self.sm
        v = sm["v"]
        kind, sec, J, pair, ob, W = u
        q = self.idx % 2
        dqs = sm["dqs0"] if q == 0 else sm["dqs1"]
        Rg = self.ld_region[self.idx]
        pk, sc, srcp, srcs, dst, R = self._views(u, q, Rg)
        tmp = Rg.dtmp[0:R, 0:2 * W]
        sgp = Rg.dsg[0:R, 0:W]
        stg = Rg.stg[q][0:R, 0:2 * W]
        t7 = _deq_ops(p, v, [self.ld_tok[self.idx], self.st_chain[q]] +
                      self.guard, pk, sc, stg, tmp, sgp, pair)
        self.t6_chain[q] = (t7[0], t7[1] - 1)
        stok = p.op("sync", lambda e, dst=dst, stg=stg: e.dma_start(
            out=dst, in_=stg), waits=[t7], inc=dqs, dma=True)
        self.st_chain[q] = stok
        key = ("gu", sec, J * 4 + pair) if kind == "gu" else ("dn", sec, pair)
        self.store_tok.setdefault(key, []).append(stok)


def _emit_s3(p, sm, A, D, entry, mtok, hook=None, xn1_t=None):
    t, v, s, d, g = (sm[k] for k in ("t", "v", "s", "d", "g"))
    g01 = (sm["g0"], sm["g1"])
    g23 = (sm["g2"], sm["g3"])
    d01 = (sm["d0"], sm["d1"])

    xn1 = A.A2[:]
    qT = A.A3[:]
    aU = A.A4[:]
    ksl = A.A5[:, 0:8192]
    vsl = A.A5[:, 8192:16384]
    wsl = A.A1[:, 0:2048].bitcast(BF16)
    ptb = A.A1[:, 2048:3072].bitcast(BF16)
    rb = A.A1[:, 3072:4096]
    rr = A.A1[0:1, 4096:5120]
    wsl2 = A.A5[:, 0:4096]
    stg = A.A5[:, 4096:6144]
    cst, ones_b, ones1_f = A.cst, A.ones_b, A.ones1_f
    ps_w0, ps_w1 = A.ps_w0, A.ps_w1
    ps_at = A.ps_r
    ps_sm = [A.ps_tr[:].bitcast(F32), A.ps_tr2[:].bitcast(F32)]

    if xn1_t is None:
        tok_xn = p.op("sync", lambda e: e.dma_start(out=xn1, in_=D.xn1),
                      waits=entry, inc=d, dma=True)
    else:
        tok_xn = None

    wl = [None, None]
    wu = [None, None]
    evt = [None, None]
    for ot in range(NT):
        sl = ot % 2
        wl[sl] = p.op("gpsimd", lambda e, ot=ot, sl=sl: e.dma_start(
            out=wsl[:, sl * DM:(sl + 1) * DM], in_=D.wq[ot]),
            waits=[wu[sl]] if wu[sl] else entry, inc=g01[sl], dma=True)
        ps = (ps_w0, ps_w1)[sl]
        mm = None
        for hf in range(2):
            for kt in range(NT):
                mm = p.op("tensor", lambda e, ps=ps, sl=sl, hf=hf, kt=kt:
                          e.matmul(
                    ps[:, hf * HFW:(hf + 1) * HFW],
                    wsl[:, sl * DM + kt * P: sl * DM + (kt + 1) * P],
                    xn1[:, kt * TOK + hf * HFW: kt * TOK + (hf + 1) * HFW],
                    start=(kt == 0), stop=(kt == NT - 1)),
                    waits=[wl[sl],
                           xn1_t[kt] if xn1_t else tok_xn, evt[sl]] +
                          (entry if ot < 2 else []),
                    inc=(t if (hf == 1 and kt == NT - 1) else None))
        wu[sl] = mm
        evt[sl] = p.op("scalar", lambda e, ps=ps, ot=ot: e.copy(
            qT[:, ot * TOK:(ot + 1) * TOK], ps[:, 0:1024]),
            waits=[mm] + (entry if ot < 2 else []), inc=s)
    q_done = [evt[0], evt[1]]

    kl = [None, None]
    ku = [None, None]
    aU_tok = None
    prev_aU = None
    expf = {0: evt[0], 1: evt[1]}
    smf = None
    rbe = None
    for h in range(NH):
        sl = h % 2
        p.op("gpsimd", lambda e, h=h, sl=sl: e.dma_start(
            out=ksl[:, sl * KV:(sl + 1) * KV], in_=D.kvkT[h]),
            waits=[ku[sl]] if ku[sl] else entry, inc=g23[sl], dma=True)
        vl = p.op("gpsimd", lambda e, h=h, sl=sl: e.dma_start(
            out=vsl[:, sl * KV:(sl + 1) * KV].rearrange(
                "p (kc dd) -> p kc dd", dd=P),
            in_=D.kvv[h].rearrange("(kc p) dd -> p kc dd", p=P)),
            inc=g23[sl], dma=True)
        kl[sl] = vl
        mm_last = None
        extok = [None, None]
        avtok = [None, None]

        def _emit_av(kc):
            nonlocal mm_last
            ksl2 = kc % 2
            for hf in range(2):
                mm_last = p.op("tensor", lambda e, sl=sl, kc=kc, ksl2=ksl2,
                               hf=hf: e.matmul(
                    ps_at[:, hf * HFW:(hf + 1) * HFW],
                    vsl[:, sl * KV + kc * P: sl * KV + (kc + 1) * P],
                    ptb[:, ksl2 * 1024 + hf * HFW:
                        ksl2 * 1024 + (hf + 1) * HFW],
                    start=(kc == 0), stop=(kc == 31)),
                    waits=[extok[ksl2], kl[sl]] +
                          ([prev_aU] if kc == 0 else []),
                    inc=(t if (hf == 1 and kc == 31) else None))
            for hf in range(2):
                p.op("tensor", lambda e, ksl2=ksl2, hf=hf, kc=kc: e.matmul(
                    ps_sm[hf][0:1, 0:HFW], ones_b[:, 0:1],
                    ptb[:, ksl2 * 1024 + hf * HFW:
                        ksl2 * 1024 + (hf + 1) * HFW],
                    start=(kc == 0), stop=(kc == 31)),
                    waits=[mtok] + ([smf] if kc == 0 else []),
                    inc=(t if (hf == 1 and kc == 31) else None))
            avtok[ksl2] = mm_last

        for kc in range(32):
            ksl2 = kc % 2
            ps = (ps_w0, ps_w1)[ksl2]
            mm = None
            for hf in range(2):
                mm = p.op("tensor", lambda e, ps=ps, h=h, sl=sl, kc=kc,
                          hf=hf: e.matmul(
                    ps[:, hf * HFW:(hf + 1) * HFW],
                    ksl[:, sl * KV + kc * P: sl * KV + (kc + 1) * P],
                    qT[:, h * TOK + hf * HFW: h * TOK + (hf + 1) * HFW],
                    start=True, stop=True),
                    waits=[kl[sl], expf[ksl2]] + q_done,
                    inc=(t if hf == 1 else None))
            ex = p.op("scalar", lambda e, ps=ps, ksl2=ksl2: e.activation(
                ptb[:, ksl2 * 1024:(ksl2 + 1) * 1024], ps[:, 0:1024],
                AF.Exp, bias=cst[:, 0:1], scale=SCL1),
                waits=[mm, mtok, avtok[ksl2]], inc=s)
            expf[ksl2] = ex
            extok[ksl2] = ex
            if kc >= 1:
                _emit_av(kc - 1)
        _emit_av(31)
        ku[sl] = mm_last
        sum_tok = (t, t.n)
        rec = None
        for hf in range(2):
            rec = p.op("vector", lambda e, hf=hf: e.reciprocal(
                rr[0:1, hf * HFW:(hf + 1) * HFW], ps_sm[hf][0:1, 0:HFW]),
                waits=[sum_tok] + (entry if h == 0 else []), inc=v)
        smf = rec
        bc = None
        for hf in range(2):
            bc = p.op("tensor", lambda e, hf=hf: e.matmul(
                ps_w0[:, hf * HFW:(hf + 1) * HFW], ones1_f[0:1, :],
                rr[0:1, hf * HFW:(hf + 1) * HFW], start=True, stop=True),
                waits=[rec, expf[0], mtok], inc=t)
        rbe = p.op("scalar", lambda e: e.copy(rb[:], ps_w0[:, 0:1024]),
                   waits=[bc, prev_aU], inc=s)
        expf[0] = rbe
        aU_tok = p.op("vector", lambda e, h=h: e.tensor_tensor(
            aU[:, h * TOK:(h + 1) * TOK], ps_at[:, 0:1024], rb[:], ALU.mult),
            waits=[rbe, mm_last] + (entry if h == 0 else []), inc=v)
        prev_aU = aU_tok
        if hook is not None:
            hook(h)

    pre_wo = [(t, t.n), (sm["v"], sm["v"].n), (s, s.n)]
    wl2 = [None, None]
    wu2 = [None, None]
    ev2 = [rbe, None]
    at_st = [None, None]
    for ot in range(NT):
        sl = ot % 2
        wl2[sl] = p.op("gpsimd", lambda e, ot=ot, sl=sl: e.dma_start(
            out=wsl2[:, sl * DM:(sl + 1) * DM], in_=D.wo[ot]),
            waits=[wu2[sl] if wu2[sl] else ku[sl]], inc=g01[sl], dma=True)
        ps = (ps_w0, ps_w1)[sl]
        mm = None
        for hf in range(2):
            for kt in range(NT):
                mm = p.op("tensor", lambda e, ps=ps, sl=sl, hf=hf, kt=kt:
                          e.matmul(
                    ps[:, hf * HFW:(hf + 1) * HFW],
                    wsl2[:, sl * DM + kt * P: sl * DM + (kt + 1) * P],
                    aU[:, kt * TOK + hf * HFW: kt * TOK + (hf + 1) * HFW],
                    start=(kt == 0), stop=(kt == NT - 1)),
                    waits=[wl2[sl], aU_tok, ev2[sl]],
                    inc=(t if (hf == 1 and kt == NT - 1) else None))
        wu2[sl] = mm
        ev2[sl] = p.op("scalar", lambda e, ps=ps, sl=sl: e.copy(
            stg[:, sl * 1024:(sl + 1) * 1024], ps[:, 0:1024]),
            waits=[mm, at_st[sl]], inc=s)
        at_st[sl] = p.op("sync", lambda e, ot=ot, sl=sl: e.dma_start(
            out=D.at1[:, ot * TOK:(ot + 1) * TOK],
            in_=stg[:, sl * 1024:(sl + 1) * 1024]),
            waits=[ev2[sl]], inc=d01[sl], dma=True)
    return {"at1_st": [at_st[0], at_st[1]], "pre_wo": pre_wo,
            "wo_end": [(t, t.n), (s, s.n)]}


def _deq_ops(p, v, waits, src_i32, src_sc, dst_bf, tmp_bf, sg_i32, pair,
             permute=False):
    mg = dst_bf.bitcast(I32)
    mg1 = mg.rearrange("p (b a) -> p a b", b=8) if permute else mg
    sg1 = (sg_i32.rearrange("p (b a) -> p a b", b=8) if permute else sg_i32)
    shl = 6 - 4 * pair
    if shl >= 0:
        t1 = p.op("vector", lambda e: e.tensor_scalar(
            mg1, src_i32, shl, 0x01C001C0, ALU.logical_shift_left,
            ALU.bitwise_and), waits=waits, inc=v)
    else:
        t1 = p.op("vector", lambda e: e.tensor_scalar(
            mg1, src_i32, -shl, 0x01C001C0, ALU.logical_shift_right,
            ALU.bitwise_and), waits=waits, inc=v)
    t2 = p.op("vector", lambda e: e.tensor_scalar(
        mg, mg, 0x3F003F00, None, ALU.add), waits=[t1], inc=v)
    t3 = p.op("vector", lambda e: e.tensor_tensor(
        dst_bf, dst_bf, src_sc, ALU.mult), waits=[t2], inc=v)
    t4 = p.op("vector", lambda e: e.scalar_tensor_tensor(
        tmp_bf, dst_bf, 2.0, src_sc, ALU.mult, ALU.subtract), waits=[t3], inc=v)
    t5 = p.op("vector", lambda e: e.tensor_tensor(
        dst_bf, dst_bf, tmp_bf, ALU.min), waits=[t4], inc=v)
    t6 = p.op("vector", lambda e: e.tensor_scalar(
        sg1, src_i32, 12 - 4 * pair, -2147450880,
        ALU.logical_shift_left, ALU.bitwise_and), waits=[t5], inc=v)
    t7 = p.op("vector", lambda e: e.tensor_tensor(
        mg, mg, sg_i32, ALU.bitwise_or), waits=[t6], inc=v)
    return t7


def _emit_s4(p, sm, A, D, entry, mtok, dq, have_xn1=False, pre=None):
    t, v, s, d, g = (sm[k] for k in ("t", "v", "s", "d", "g"))
    if pre is None:
        pre = {}
    pre_wo = pre.get("pre_wo", entry)
    wo_end = pre.get("wo_end", entry)
    at1_st = pre.get("at1_st", [])
    x1_st = pre.get("x1_st", None)
    dqfin = [(sm[k], sm[k].n) for k in ("dqs0", "dqs1", "dql0", "dql1")]

    y = A.A1[:]
    xn1 = A.A2[:]
    bufs = (A.A3[:], A.A5[:])
    hcur = A.A4[:, 0:8192]
    sg = A.A4[:, 8192:16384]
    sq = A.A3[:, 0:4096].bitcast(F32)
    ab = A.A5[:, 4096:8192].bitcast(F32)
    st = A.A5[0:1, 8192:16384].bitcast(F32)
    x1t = A.A2[:, 0:4096].bitcast(F32)
    ps_w0, ps_w1, ps_r = A.ps_w0, A.ps_w1, A.ps_r

    NB = [1024] * 5 + [512]

    dqfin2 = dqfin
    tok_xn = None
    if not have_xn1:
        tok_xn = p.op("sync", lambda e: e.dma_start(out=xn1, in_=D.xn1),
                      waits=entry, inc=d, dma=True)
    buf_free = [list(pre_wo), list(wo_end) + list(at1_st)]
    sec_i = 0
    hfree = None
    ht_tok = [None] * 8
    sg_tok = [None] * 8
    ps_free = {0: None, 1: None}
    ydone = None
    ydone_t = {}
    xn1_free = None
    for Jp in range(6):
        W = NB[Jp]
        for sect in range(2):
            sec = Jp * 2 + sect
            bq_ = sec_i % 2
            big = bufs[bq_]
            bfw = buf_free[bq_]
            bfw = bfw if isinstance(bfw, list) else [bfw]
            gsec = g if bq_ == 0 else sm["g2"]
            ld = None
            for sb in range(8):
                ld = p.op("gpsimd", lambda e, sec=sec, sb=sb, W=W, big=big:
                          e.dma_start(
                    out=big[:, sb * 2048: sb * 2048 + 2 * W],
                    in_=D.gw[sec][:, sb * 2048: sb * 2048 + 2 * W]),
                    waits=(bfw if sb == 0 else []) +
                          dq.store_tok[("gu", sec, sb)],
                    inc=gsec, dma=True)
            mm_done = None
            nrow = W // 8
            for pp in range(8):
                psel = pp % 2
                ps = (ps_w0, ps_w1)[psel]
                mm = None
                for hf in range(2):
                    for T in range(NT):
                        J, pl = T // 8, T % 8
                        pair, half = pl % 4, pl // 4
                        off = (J * 4 + pair) * 2048 + 2 * pp + half
                        xw = [ld, tok_xn] if tok_xn else [ld]
                        if T == 0 and hf == 0:
                            xw.append(ps_free[psel])
                            if sec_i == 0:
                                xw += list(wo_end)
                        xw = [w_ for w_ in xw if w_ is not None]
                        mm = p.op("tensor", lambda e, ps=ps, off=off, big=big,
                                  nrow=nrow, T=T, hf=hf: e.matmul(
                            ps[0:nrow, hf * HFW:(hf + 1) * HFW],
                            big[:, off: off + 16 * (nrow - 1) + 1: 16],
                            xn1[:, T * TOK + hf * HFW:
                                T * TOK + (hf + 1) * HFW],
                            start=(T == 0), stop=(T == NT - 1)),
                            waits=xw,
                            inc=(t if (hf == 1 and T == NT - 1) else None))
                if sect == 0:
                    sg_tok[pp] = p.op("scalar", lambda e, ps=ps, pp=pp,
                                      nrow=nrow: e.activation(
                        sg[0:nrow, pp * TOK:(pp + 1) * TOK],
                        ps[0:nrow, 0:1024], AF.Silu),
                        waits=[mm, ht_tok[pp]], inc=s)
                    ps_free[psel] = sg_tok[pp]
                else:
                    ht_tok[pp] = p.op("vector", lambda e, ps=ps, pp=pp,
                                      nrow=nrow: e.tensor_tensor(
                        hcur[0:nrow, pp * TOK:(pp + 1) * TOK],
                        ps[0:nrow, 0:1024],
                        sg[0:nrow, pp * TOK:(pp + 1) * TOK], ALU.mult),
                        waits=[mm, sg_tok[pp], hfree], inc=v)
                    ps_free[psel] = ht_tok[pp]
                mm_done = mm
            buf_free[bq_] = [mm_done]
            if Jp == 5 and sect == 1:
                xn1_free = mm_done
            sec_i += 1

        R = 128 if Jp < 5 else 64
        bq_ = sec_i % 2
        dbig = bufs[bq_]
        bfw = buf_free[bq_]
        bfw = bfw if isinstance(bfw, list) else [bfw]
        gsec = g if bq_ == 0 else sm["g2"]
        dld = None
        for pair in range(4):
            dld = p.op("gpsimd", lambda e, Jp=Jp, pair=pair, R=R, dbig=dbig:
                       e.dma_start(
                out=dbig[0:R, pair * 4096:(pair + 1) * 4096],
                in_=D.dw[Jp][0:R, pair * 4096:(pair + 1) * 4096]),
                waits=(bfw if pair == 0 else []) +
                      dq.store_tok[("dn", Jp, pair)],
                inc=gsec, dma=True)
        for ot in range(NT):
            psel = ot % 2
            ps = (ps_w0, ps_w1)[psel]
            mm = None
            for hf in range(2):
                for pl in range(8):
                    pair, half = pl % 4, pl // 4
                    off = pair * 4096 + 2 * (ot * P) + half
                    mm = p.op("tensor", lambda e, ps=ps, off=off, pl=pl,
                              hf=hf, R=R, dbig=dbig: e.matmul(
                        ps[:, hf * HFW:(hf + 1) * HFW],
                        dbig[0:R, off: off + 2 * (P - 1) + 1: 2],
                        hcur[0:R, pl * TOK + hf * HFW:
                             pl * TOK + (hf + 1) * HFW],
                        start=(pl == 0), stop=(pl == 7)),
                        waits=[dld, ht_tok[pl],
                               ps_free[psel] if (pl == 0 and hf == 0)
                               else None],
                        inc=(t if (hf == 1 and pl == 7) else None))
            for hf in range(2):
                if Jp == 0:
                    ydone = p.op("vector", lambda e, ps=ps, ot=ot, hf=hf:
                                 e.tensor_copy(
                        y[:, ot * TOK + hf * HFW: ot * TOK + (hf + 1) * HFW],
                        ps[:, hf * HFW:(hf + 1) * HFW]),
                        waits=[mm] + (dqfin2 if ot < 2 else []) +
                              (list(pre_wo) if ot < 2 else []), inc=v)
                else:
                    ydone = p.op("vector", lambda e, ps=ps, ot=ot, hf=hf:
                                 e.tensor_tensor(
                        y[:, ot * TOK + hf * HFW: ot * TOK + (hf + 1) * HFW],
                        y[:, ot * TOK + hf * HFW: ot * TOK + (hf + 1) * HFW],
                        ps[:, hf * HFW:(hf + 1) * HFW], ALU.add),
                        waits=[mm], inc=v)
            if Jp == 5:
                ydone_t[ot] = ydone
            ps_free[psel] = ydone
        buf_free[bq_] = [(t, t.n)]
        hfree = (t, t.n)
        sec_i += 1

    rms_tok, _, _rt = _emit_ln(p, sm, y, y, sq, st, ab, A.cst, A.ones_f,
                          A.ones1_f, ps_r[:], ps_w0[:], ps_w1[:], eps_col=2,
                          x_ready=[mtok],
                          x_ready_t=[ydone_t[T] for T in range(NT)],
                          ps_free=[(t, t.n), (v, v.n)], rms=True,
                          ones_b=A.ones_b)
    x1t4 = A.A2[:, 0:8192].bitcast(F32)
    at1t = A.A2[:, 8192:12288]
    sq4 = A.A3[:, 0:8192].bitcast(F32)
    xsem = [d, sm["d0"], sm["d1"], sm["d2"]]
    asem = [sm["dql0"], sm["dql1"], sm["dqs0"], sm["dqs1"]]
    x1l = [None] * 4
    a1l = [None] * 4
    stf4 = [None] * 4
    stf = None
    x1st = pre.get("x1_st", None)

    def _tail_load(T, gate):
        sl = T % 4
        x1l[sl] = p.op("sync", lambda e, T=T, sl=sl: e.dma_start(
            out=x1t4[:, sl * 1024:(sl + 1) * 1024],
            in_=D.x1[:, T * TOK:(T + 1) * TOK]),
            waits=gate, inc=xsem[sl], dma=True)
        a1l[sl] = p.op("sync", lambda e, T=T, sl=sl: e.dma_start(
            out=at1t[:, sl * 1024:(sl + 1) * 1024],
            in_=D.at1[:, T * TOK:(T + 1) * TOK]),
            waits=gate, inc=asem[sl], dma=True)

    for T in range(4):
        _tail_load(T, [xn1_free, x1st] + list(at1_st))
    for T in range(NT):
        sl = T % 4
        t1 = p.op("vector", lambda e, T=T, sl=sl: e.tensor_tensor(
            sq4[:, sl * 1024:(sl + 1) * 1024], y[:, T * TOK:(T + 1) * TOK],
            ab[:, 0:1024], ALU.mult), waits=[rms_tok], inc=v)
        t2 = p.op("vector", lambda e, T=T, sl=sl: e.tensor_tensor(
            sq4[:, sl * 1024:(sl + 1) * 1024],
            sq4[:, sl * 1024:(sl + 1) * 1024],
            x1t4[:, sl * 1024:(sl + 1) * 1024], ALU.add),
            waits=[t1, x1l[sl]], inc=v)
        stf = p.op("vector", lambda e, T=T, sl=sl: e.tensor_tensor(
            y[:, T * TOK:(T + 1) * TOK], sq4[:, sl * 1024:(sl + 1) * 1024],
            at1t[:, sl * 1024:(sl + 1) * 1024], ALU.add),
            waits=[t2, a1l[sl]], inc=v)
        stf4[sl] = stf
        if T + 4 < NT:
            _tail_load(T + 4, [stf])
    sto = p.op("sync", lambda e: e.dma_start(out=D.xout, in_=y),
               waits=[stf], inc=d, dma=True)
    return {"out_st": sto}


_DRAM_SPECS = [
    ("dmaskT", "msk", [P, NH * P], BF16), ("dqb", "dqb", [P, NH * P], BF16),
    ("dk", "dk", [P, NH], F32), ("pk", "pk", [P, P], F32),
    ("ident", "idn", [P, P], BF16), ("bqkv", "bq", [P, 48], F32),
    ("xpi", "x", [P, NT * TOK], F32), ("xpre", "xpre", [P, NT * TOK], F32),
    ("wqkv", "wqkv", [48, P, DM], F32), ("wout", "wout", [NT, P, DM], F32),
    ("bout", "bo", [P, NT], F32), ("p1", "p1", [NFT, P, DM], F32),
    ("p1b", "p1b", [P, NFT], F32), ("p2g", "p2g", [8, P, NT * 1024], F32),
    ("p2b", "p2b", [P, NT], F32), ("wq", "wq", [NT, P, DM], F32),
    ("kvkT", "kvkT", [NH, P, KV], F32), ("kvv", "kvv", [NH, KV, P], F32),
    ("wo", "wo", [NT, P, DM], F32), ("gupk", "gupk", [256, GU], I32),
    ("gusc", "gusc", [256, 2 * GU], F16), ("dnpk", "dnpk", [704, DM], I32),
    ("dnsc", "dnsc", [704, 2 * DM], F16),
]
_HANDOFF_SPECS = [
    ("attnT", "attn", [P, NT * TOK], BF16), ("xnT", "xn", [P, NT * TOK], BF16),
    ("x1T", "x1", [P, NT * TOK], F32), ("xn1T", "xn1", [P, NT * TOK], BF16),
    ("at1T", "at1", [P, NT * TOK], BF16),
]


def _mk_dram(nc, handoff_kind, out_names=()):
    D = NS()
    for name, attr, shape, dty in _DRAM_SPECS:
        kind = "ExternalOutput" if name in out_names else "ExternalInput"
        setattr(D, attr, nc.dram_tensor(name, shape, dty, kind=kind).ap())
    for name, attr, shape, dty in _HANDOFF_SPECS:
        kind = ("ExternalOutput" if name in out_names else
                ("ExternalInput" if handoff_kind == "io" else "Internal"))
        setattr(D, attr, nc.dram_tensor(name, shape, dty, kind=kind).ap())
    D.x1p = nc.dram_tensor("x1p", [P, NT * TOK], F32, kind="Internal").ap()
    D.gw = nc.dram_tensor("gwd", [12, P, NT * TOK], BF16, kind="Internal").ap()
    D.dw = nc.dram_tensor("dwd", [6, P, NT * TOK], BF16, kind="Internal").ap()
    D.xout = nc.dram_tensor("xout", [P, NT * TOK], F32,
                            kind="ExternalOutput" if "xout" in out_names or
                            handoff_kind == "fused" else "Internal").ap()
    return D


def _build_fused():
    nc = bass.Bass("TRN2", target_bir_lowering=False)
    D = _mk_dram(nc, "fused")
    with ExitStack() as ctx:
        e_ = ctx.enter_context
        A = _alloc(nc, e_)
        blk = e_(nc.Block())
        p = Prog(nc, ctx)
        sm = _mk_sems(p)
        mtok = _emit_consts(p, sm, A)
        r1 = _emit_s1(p, sm, A, D, [], mtok)
        e1 = _stage_end(sm)
        dq = _DeqEarly(p, sm, A, D, e1)
        st2 = {"init": False}

        def s2hook(gb):
            if not st2["init"]:
                dq.set_region("a3", [(sm["t"], sm["t"].n),
                                     (sm["d0"], sm["d0"].n),
                                     (sm["d1"], sm["d1"].n)])
                st2["init"] = True
            dq.take(11)

        r2 = _emit_s2(p, sm, A, D, e1, mtok, hook=s2hook,
                      a1_free=r1["a1_free"])
        dq.drain()
        e2 = _stage_end(sm)
        st3 = {"init": False}

        def s3hook(h):
            if not st3["init"]:
                dq.set_region("a1", e2)
                st3["init"] = True
            dq.take(4)

        e2s = list(r2["a1f2"]) + [
            (sm[k], sm[k].n) for k in ("dqs0", "dqs1", "dql0", "dql1", "v")]
        r3 = _emit_s3(p, sm, A, D, e2s, mtok, hook=s3hook,
                      xn1_t=r2["ln1_t"])
        dq.finish()
        e3 = _stage_end(sm)
        r4 = _emit_s4(p, sm, A, D, e3, mtok, dq, have_xn1=True,
                      pre={"pre_wo": r3["pre_wo"], "wo_end": r3["wo_end"],
                           "at1_st": r3["at1_st"], "x1_st": r2["x1_st"]})
        p.op("sync", lambda e: e.nop(), waits=[r4["out_st"]])
        p.emit(blk)
    return nc


def _build_stage_n(n):
    out_names = {1: ("attnT", "xnT"), 2: ("x1T", "xn1T"), 3: ("at1T",),
                 4: ("xout",)}[n]
    nc = bass.Bass("TRN2", target_bir_lowering=False)
    D = _mk_dram(nc, "io", out_names)
    emit = {1: _emit_s1, 2: _emit_s2, 3: _emit_s3, 4: _emit_s4}[n]
    with ExitStack() as ctx:
        e_ = ctx.enter_context
        A = _alloc(nc, e_)
        blk = e_(nc.Block())
        p = Prog(nc, ctx)
        sm = _mk_sems(p)
        mtok = _emit_consts(p, sm, A)
        if n == 4:
            dq = _DeqEarly(p, sm, A, D, [])
            dq.set_region("a1", [])
            dq.finish()
            r = emit(p, sm, A, D, _stage_end(sm), mtok, dq)
        else:
            r = emit(p, sm, A, D, [], mtok)
        toks = []
        for vv in r.values():
            toks += vv if isinstance(vv, list) else [vv]
        p.op("sync", lambda e: e.nop(), waits=toks)
        p.emit(blk)
    return nc


def _build_s1():
    return _build_stage_n(1)


def _build_s2():
    return _build_stage_n(2)


def _build_s3():
    return _build_stage_n(3)


def _build_s4():
    return _build_stage_n(4)


_NC_CACHE = {}


def _get(name, builder):
    if name not in _NC_CACHE:
        _NC_CACHE[name] = builder()
    return _NC_CACHE[name]


def _in_map(hp, c, extra):
    im = {"xpi": hp["x_cores"][c], "xpre": hp["xpre_cores"][c],
          "wqkv": hp["wqkv"], "bqkv": hp["bqkv"], "dmaskT": hp["dmaskT"],
          "dqb": hp["dqb"], "dk": hp["dk"],
          "pk": hp["pk_odd"] if c % 2 else hp["pk_even"],
          "ident": hp["ident"], "wout": hp["wout"], "bout": hp["bout"],
          "p1": hp["p1"], "p1b": hp["p1b"], "p2g": hp["p2g"],
          "p2b": hp["p2b"], "wq": hp["wq"], "kvkT": hp["kvkT"],
          "kvv": hp["kvv"], "wo": hp["wo"], "gupk": hp["gupk"],
          "gusc": hp["gusc"], "dnpk": hp["dnpk"], "dnsc": hp["dnsc"]}
    im.update(extra)
    return im


def kernel(**inputs):
    hp = _host_prep(inputs)
    cores = list(range(NCORE))
    nc = _get("fused", _build_fused)
    im = [_in_map(hp, c, {}) for c in cores]
    res = run_bass_kernel_spmd(nc, im, core_ids=cores)
    out = np.empty((4, 2048, DM), np.float32)
    for c in cores:
        b, half = c // 2, c % 2
        out[b, half * TOK:(half + 1) * TOK] = _unfold_out(res.results[c]["xout"])
    return out


# version 15
# speedup vs baseline: 1.0894x; 1.0894x over previous
import math
from contextlib import ExitStack
from types import SimpleNamespace as NS

import numpy as np
import ml_dtypes

import concourse.bass as bass
import concourse.mybir as mybir
from concourse.bass_utils import run_bass_kernel_spmd

F32 = mybir.dt.float32
BF16 = mybir.dt.bfloat16
F16 = mybir.dt.float16
I32 = mybir.dt.int32
AF = mybir.ActivationFunctionType
ALU = mybir.AluOpType
BF = ml_dtypes.bfloat16

P = 128
DM = 2048
NT = DM // P
TOK = 1024
HFW = 512
NH = 16
HD = 128
NCH = TOK // HD
FF = 8192
NFT = FF // P
II = 5632
GU = 2 * II
KV = 4096
SCL1 = 1.0 / math.sqrt(HD)
NCORE = 8


def _dperm():
    d = np.zeros(DM, dtype=np.int64)
    for T in range(NT):
        J, p = T // 8, T % 8
        r = np.arange(P)
        d[T * P + r] = 1024 * J + 8 * r + p
    return d


_PERM = _dperm()


def _tile_lhsT(WT):
    IN, OUT = WT.shape
    L = WT.reshape(IN // P, P, OUT // P, P)
    return np.ascontiguousarray(L.transpose(2, 1, 0, 3).reshape(OUT // P, P, IN))


def _slopes(n_heads):
    start = 2.0 ** (-(2.0 ** (-(math.log2(n_heads) - 3))))
    return np.array([start ** (i + 1) for i in range(n_heads)], dtype=np.float64)


def _host_prep(inp):
    pm = _PERM
    f32 = np.float32
    out = {}

    w_qkv = np.asarray(inp["L0_attn_w_qkv"], f32)
    out["wqkv"] = _tile_lhsT(np.ascontiguousarray(w_qkv.T[pm]))
    out["bqkv"] = np.ascontiguousarray(
        np.asarray(inp["L0_attn_b_qkv"], f32).reshape(48, P).T)
    w_out = np.asarray(inp["L0_attn_w_out"], f32)
    out["wout"] = _tile_lhsT(np.ascontiguousarray(w_out.T[:, pm]))
    out["bout"] = np.ascontiguousarray(
        np.asarray(inp["L0_attn_b_out"], f32)[pm].reshape(NT, P).T)

    p1 = np.asarray(inp["L0_ffn_proj1"], f32)
    out["p1"] = _tile_lhsT(np.ascontiguousarray(p1.T[pm]))
    out["p1b"] = np.ascontiguousarray(
        np.asarray(inp["L0_ffn_proj1_bias"], f32).reshape(NFT, P).T)
    p2 = np.asarray(inp["L0_ffn_proj2"], f32)
    A = np.ascontiguousarray(p2.T[:, pm])
    A4 = A.reshape(8, 8, P, NT, P)
    out["p2g"] = np.ascontiguousarray(
        A4.transpose(0, 2, 3, 1, 4).reshape(8, P, NT * 1024))
    out["p2b"] = np.ascontiguousarray(
        np.asarray(inp["L0_ffn_proj2_bias"], f32)[pm].reshape(NT, P).T)

    wq = np.asarray(inp["L1_attn_Wq"], f32)
    out["wq"] = _tile_lhsT(np.ascontiguousarray(wq.T[pm]))
    wo = np.asarray(inp["L1_attn_Wo"], f32)
    out["wo"] = _tile_lhsT(np.ascontiguousarray(wo.T[:, pm]))
    kvk = np.asarray(inp["L1_attn_kv_k"], f32).reshape(KV, NH, HD)
    out["kvkT"] = np.ascontiguousarray(kvk.transpose(1, 2, 0))
    kvv = np.asarray(inp["L1_attn_kv_v"], f32).reshape(KV, NH, HD)
    out["kvv"] = np.ascontiguousarray(kvv.transpose(1, 0, 2))

    gupk = np.asarray(inp["L1_ffn_gate_up_packed"])
    out["gupk"] = np.ascontiguousarray(gupk.T)
    gusc = np.asarray(inp["L1_ffn_gate_up_scales"])
    out["gusc"] = np.ascontiguousarray(
        np.repeat(np.repeat(gusc, 16, axis=0), 2, axis=1))
    dnpk = np.asarray(inp["L1_ffn_down_packed"])
    out["dnpk"] = np.ascontiguousarray(dnpk[pm].T)
    dnsc = np.asarray(inp["L1_ffn_down_scales"])
    out["dnsc"] = np.ascontiguousarray(
        np.repeat(np.repeat(dnsc[:, pm], 16, axis=0), 2, axis=1))

    lam = np.exp(-_slopes(NH))
    pos = np.arange(HD, dtype=np.float64)
    decay_q = lam[:, None] ** (pos + 1.0)
    decay_k = lam[:, None] ** (HD - 1.0 - pos)
    diff = pos[:, None] - pos[None, :]
    dmask = np.where(diff >= 0, lam[:, None, None] ** np.maximum(diff, 0), 0.0)
    lam_c = lam ** HD
    out["dmaskT"] = np.ascontiguousarray(
        dmask.transpose(2, 0, 1).reshape(P, NH * P)).astype(BF)
    out["dqb"] = np.ascontiguousarray(
        np.broadcast_to(decay_q[None, :, :], (P, NH, P))
        .reshape(P, NH * P)).astype(BF)
    out["dk"] = np.ascontiguousarray(decay_k.T.astype(f32))
    pk = np.zeros((P, P), dtype=np.float64)
    for cp in range(8):
        for h in range(NH):
            pk[:, cp * 16 + h] = decay_k[h] * lam_c[h] ** (7 - cp)
    out["pk_odd"] = pk.astype(f32)
    out["pk_even"] = np.zeros((P, P), dtype=f32)
    out["ident"] = np.eye(P, dtype=BF)

    x = np.asarray(inp["L0_attn_x"], f32)

    def fold(sl):
        a = np.ascontiguousarray(sl.T[pm])
        return np.ascontiguousarray(
            a.reshape(NT, P, TOK).transpose(1, 0, 2).reshape(P, NT * TOK))
    xs, xps = [], []
    for c in range(NCORE):
        b, half = c // 2, c % 2
        xs.append(fold(x[b, half * TOK:(half + 1) * TOK]))
        xps.append(fold(x[b, 0:TOK]) if half == 1 else xs[-1])
    out["x_cores"] = xs
    out["xpre_cores"] = xps
    return out


def _unfold_out(res_c):
    a = np.asarray(res_c, np.float32).reshape(P, NT, TOK).transpose(1, 0, 2)
    a = a.reshape(DM, TOK)
    nat = np.empty((TOK, DM), np.float32)
    nat[:, _PERM] = a.T
    return nat


class _Sem:
    __slots__ = ("h", "n")


class Prog:
    ENGINES = ("sync", "gpsimd", "tensor", "vector", "scalar")

    def __init__(self, nc, ctx):
        self.nc = nc
        self.ctx = ctx
        self.ops = {e: [] for e in self.ENGINES}

    def sem(self, name):
        s = _Sem()
        s.h = self.ctx.enter_context(self.nc.semaphore(name))
        s.n = 0
        return s

    def op(self, eng, fn, waits=(), inc=None, dma=False):
        k = 16 if dma else 1
        w = tuple(x for x in waits if x is not None)
        self.ops[eng].append((fn, w, inc, k))
        if inc is not None:
            inc.n += k
            return (inc, inc.n)
        return None

    def emit(self, block):
        for e in self.ENGINES:
            items = self.ops[e]
            if not items:
                continue

            def body(eng, items=items):
                seen = {}
                for fn, waits, inc, k in items:
                    for (s, v) in waits:
                        if seen.get(id(s), -1) >= v:
                            continue
                        eng.wait_ge(s.h, v)
                        seen[id(s)] = v
                    inst = fn(eng)
                    if inc is not None:
                        inst.then_inc(inc.h, k)
            getattr(block, e)(body)


def _mk_sems(p):
    return {k: p.sem(k) for k in
            ("t", "v", "s", "d", "g", "m", "g0", "g1", "d0", "d1", "g2", "g3",
             "dql0", "dql1", "dqs0", "dqs1", "d2", "g4")}


def _stage_end(sm):
    return [(s, s.n) for s in sm.values()]


def _emit_ln(p, sm, x_sb, xn_sb, sq_sb, st_sb, ab_sb, cst, ones_f, ones1_f,
             ps_r, ps_w0, ps_w1, eps_col, x_ready, ps_free=(), rms=False,
             ones_b=None, x_ready_t=None, x_bf16=False):
    t, v, s = sm["t"], sm["v"], sm["s"]
    xr = list(x_ready)
    pf = list(ps_free)
    sqb = sq_sb.bitcast(BF16)
    sq_tok = [None, None]
    mm_tok = None
    for T in range(NT):
        sl = T % 2
        xrT = xr + ([x_ready_t[T]] if x_ready_t else [])
        tok_sq = p.op("scalar", lambda e, T=T, sl=sl: e.activation(
            sqb[:, sl * 2048: sl * 2048 + 1024],
            x_sb[:, T * 1024:(T + 1) * 1024],
            AF.Square), waits=xrT + [sq_tok[sl]], inc=s)
        if not rms:
            ones_sum = ones_b if x_bf16 else ones_f
            for hf in range(2):
                p.op("tensor", lambda e, T=T, hf=hf, ones_sum=ones_sum:
                     e.matmul(
                    ps_r[0:1, hf * HFW:(hf + 1) * HFW], ones_sum[:, 0:1],
                    x_sb[:, T * 1024 + hf * HFW: T * 1024 + (hf + 1) * HFW],
                    start=(T == 0), stop=(T == NT - 1)),
                    waits=xrT + (pf if T == 0 else []))
        for hf in range(2):
            mm_tok = p.op("tensor", lambda e, T=T, hf=hf, sl=sl: e.matmul(
                ps_w1[0:1, hf * HFW:(hf + 1) * HFW], ones_b[:, 0:1],
                sqb[:, sl * 2048 + hf * HFW: sl * 2048 + (hf + 1) * HFW],
                start=(T == 0), stop=(T == NT - 1)),
                waits=[tok_sq] + (pf if T == 0 else []), inc=t)
        sq_tok[sl] = mm_tok
    inv = 1.0 / DM
    if rms:
        var_tok = p.op("vector", lambda e: e.tensor_scalar(
            st_sb[0:1, 3072:4096], ps_w1[0:1, 0:1024], inv, None, ALU.mult),
            waits=[mm_tok], inc=v)
    else:
        tok_m = p.op("vector", lambda e: e.tensor_scalar(
            st_sb[0:1, 0:1024], ps_r[0:1, 0:1024], inv, None, ALU.mult),
            waits=[mm_tok], inc=v)
        tok_q = p.op("vector", lambda e: e.tensor_scalar(
            st_sb[0:1, 3072:4096], ps_w1[0:1, 0:1024], inv, None, ALU.mult),
            waits=[mm_tok], inc=v)
        tok_mm = p.op("vector", lambda e: e.tensor_tensor(
            st_sb[0:1, 2048:3072], st_sb[0:1, 0:1024], st_sb[0:1, 0:1024],
            ALU.mult), waits=[tok_m], inc=v)
        var_tok = p.op("vector", lambda e: e.tensor_tensor(
            st_sb[0:1, 3072:4096], st_sb[0:1, 3072:4096], st_sb[0:1, 2048:3072],
            ALU.subtract), waits=[tok_q, tok_mm], inc=v)
    tok_ln = p.op("scalar", lambda e: e.activation(
        st_sb[0:1, 2048:3072], st_sb[0:1, 3072:4096], AF.Ln,
        bias=cst[0:1, eps_col:eps_col + 1]), waits=[var_tok] + xr, inc=s)
    p.ln_ps_done = [var_tok]
    tok_rs = p.op("scalar", lambda e: e.activation(
        st_sb[0:1, 1024:2048], st_sb[0:1, 2048:3072], AF.Exp, scale=-0.5),
        waits=[tok_ln], inc=s)
    bc_tok = None
    for hf in range(2):
        bc_tok = p.op("tensor", lambda e, hf=hf: e.matmul(
            ps_w0[:, hf * HFW:(hf + 1) * HFW], ones1_f[0:1, :],
            st_sb[0:1, 1024 + hf * HFW:1024 + (hf + 1) * HFW],
            start=True, stop=True), waits=[tok_rs] + xr + pf, inc=t)
    a_tok = p.op("scalar", lambda e: e.copy(ab_sb[:, 0:1024], ps_w0[:, 0:1024]),
                 waits=[bc_tok], inc=s)
    if rms:
        return a_tok, a_tok, None
    tok_b = p.op("vector", lambda e: e.scalar_tensor_tensor(
        st_sb[0:1, 0:1024], st_sb[0:1, 0:1024], -1.0, st_sb[0:1, 1024:2048],
        ALU.mult, ALU.mult), waits=[tok_rs, tok_m], inc=v)
    bc2_tok = None
    for hf in range(2):
        bc2_tok = p.op("tensor", lambda e, hf=hf: e.matmul(
            ps_w0[:, hf * HFW:(hf + 1) * HFW], ones1_f[0:1, :],
            st_sb[0:1, hf * HFW:(hf + 1) * HFW],
            start=True, stop=True), waits=[tok_b, a_tok], inc=t)
    b_tok = p.op("scalar", lambda e: e.copy(ab_sb[:, 1024:2048], ps_w0[:, 0:1024]),
                 waits=[bc2_tok], inc=s)
    p.ln_ps_done = [var_tok, b_tok]
    done = None
    done_t = []
    for T in range(NT):
        sl = T % 2
        t1 = p.op("vector", lambda e, T=T, sl=sl: e.tensor_tensor(
            sq_sb[:, sl * 1024:(sl + 1) * 1024], x_sb[:, T * 1024:(T + 1) * 1024],
            ab_sb[:, 0:1024], ALU.mult), waits=[a_tok, b_tok], inc=v)
        done = p.op("vector", lambda e, T=T, sl=sl: e.tensor_tensor(
            xn_sb[:, T * 1024:(T + 1) * 1024], sq_sb[:, sl * 1024:(sl + 1) * 1024],
            ab_sb[:, 1024:2048], ALU.add), waits=[t1], inc=v)
        done_t.append(done)
    return done, a_tok, done_t


def _alloc(nc, e_):
    A = NS()
    A.A1 = e_(nc.sbuf_tensor("A1", [P, NT * TOK], F32))
    A.A2 = e_(nc.sbuf_tensor("A2", [P, NT * TOK], BF16))
    A.A3 = e_(nc.sbuf_tensor("A3", [P, NT * TOK], BF16))
    A.A4 = e_(nc.sbuf_tensor("A4", [P, NT * TOK], BF16))
    A.A5 = e_(nc.sbuf_tensor("A5", [P, NT * TOK], BF16))
    A.cst = e_(nc.sbuf_tensor("cstS", [P, 4], F32))
    A.ones_f = e_(nc.sbuf_tensor("onesS", [P, 1], F32))
    A.ones1_f = e_(nc.sbuf_tensor("ones1S", [1, P], F32))
    A.ones_b = e_(nc.sbuf_tensor("onebS", [P, 1], BF16))
    A.msk = e_(nc.sbuf_tensor("mskS", [P, NH * P], BF16))
    A.dqb = e_(nc.sbuf_tensor("dqbS", [P, NH * P], BF16))
    A.dk = e_(nc.sbuf_tensor("dkS", [P, NH], F32))
    A.pk = e_(nc.sbuf_tensor("pkS", [P, P], F32))
    A.idn = e_(nc.sbuf_tensor("idnS", [P, P], BF16))
    A.bq = e_(nc.sbuf_tensor("bqS", [P, 48], F32))
    A.vph = e_(nc.sbuf_tensor("vphS", [P, 2 * TOK], BF16))
    A.bo = e_(nc.sbuf_tensor("boS", [P, NT], F32))
    A.p1b = e_(nc.sbuf_tensor("p1bS", [P, NFT], F32))
    A.p2b = e_(nc.sbuf_tensor("p2bS", [P, NT], F32))
    A.ps_w0 = e_(nc.psum_tensor("ps_w0", [P, 1024], F32))
    A.ps_w1 = e_(nc.psum_tensor("ps_w1", [P, 1024], F32))
    A.ps_r = e_(nc.psum_tensor("ps_r", [P, 1024], F32))
    A.ps_tr = e_(nc.psum_tensor("ps_tr", [P, 1024], BF16))
    A.ps_tr2 = e_(nc.psum_tensor("ps_tr2", [P, 1024], BF16))
    return A


def _emit_consts(p, sm, A):
    m = sm["m"]
    p.op("gpsimd", lambda e: e.memset(A.cst[:, 0:1], -16.0), inc=m)
    p.op("gpsimd", lambda e: e.memset(A.cst[:, 1:2], 1e-5), inc=m)
    p.op("gpsimd", lambda e: e.memset(A.cst[:, 2:3], 1e-6), inc=m)
    p.op("gpsimd", lambda e: e.memset(A.ones_f[:], 1.0), inc=m)
    p.op("gpsimd", lambda e: e.memset(A.ones1_f[:], 1.0), inc=m)
    tok = p.op("gpsimd", lambda e: e.memset(A.ones_b[:], 1.0), inc=m)
    return tok


def _emit_s1(p, sm, A, D, entry, mtok):
    t, v, s, d, g = (sm[k] for k in ("t", "v", "s", "d", "g"))
    g01 = (sm["g0"], sm["g1"])
    d01 = (sm["d0"], sm["d1"])
    lam_c_vals = (np.exp(-_slopes(NH)) ** HD).astype(np.float64)

    xf = A.A1[:]
    xn = A.A2[:]
    xnp = A.A3[:]
    wsl = A.A4[:, 0:12288]
    attn = A.A4[:, 12288:14336]
    ktc = A.A4[:, 14336:14592]
    vtc = A.A4[:, 14592:14848]
    kpp = A.A4[:, 14848:15104]
    vpp = A.A4[:, 15104:15360]
    qp = A.A4[:, 15360:15616]
    pt = A.A4[:, 15616:15872]
    S = A.A4[:, 15872:16128].bitcast(F32)
    Sb = A.A4[:, 16128:16256]
    st = A.A4[0:1, 0:8192].bitcast(F32)
    sq = A.A5[:, 0:4096].bitcast(F32)
    ab = A.A5[:, 4096:8192].bitcast(F32)
    qh = A.A5[:, 8192:10240]
    kh = A.A5[:, 10240:12288]
    vh = A.A5[:, 12288:14336]
    kph = A.A5[:, 14336:16384]
    vph = A.vph[:]
    msk, dqb, dk, pk, idn, bq = A.msk, A.dqb, A.dk, A.pk, A.idn, A.bq
    ps_w0, ps_w1, ps_tr, ps_tr2, ps_r = (A.ps_w0, A.ps_w1, A.ps_tr, A.ps_tr2,
                                         A.ps_r)

    xpre_bf = A.A1[:, 8192:16384].bitcast(BF16)
    x_bf = A.A1[:, 0:8192].bitcast(BF16)
    for dst, srcd in ((msk[:], D.msk), (dqb[:], D.dqb), (dk[:], D.dk),
                      (pk[:], D.pk), (idn[:], D.idn), (bq[:], D.bq)):
        p.op("sync", lambda e, dst=dst, srcd=srcd:
             e.dma_start(out=dst, in_=srcd), waits=entry, inc=d, dma=True)
    tok_msk = tok_dqb = tok_dk = tok_pk = tok_id = tok_bq = (
        sm["d"], sm["d"].n)
    sq2 = A.A5[:, 8192:12288].bitcast(F32)
    st2 = A.A4[0:1, 8192:16384].bitcast(F32)
    ab2 = A.A5[:, 12288:16384].bitcast(F32)
    xp_sems = [sm["g4"], sm["g"], sm["g0"], sm["g1"]]
    xp_q = []
    for qq in range(4):
        xp_q.append(p.op("gpsimd", lambda e, qq=qq: e.dma_start(
            out=xpre_bf[:, qq * 4096:(qq + 1) * 4096],
            in_=D.xpre[:, qq * 4096:(qq + 1) * 4096]),
            waits=entry if qq == 0 else [], inc=xp_sems[qq], dma=True))
    xb_sems = [sm["g2"], sm["g3"]]
    xb_h = []
    for qq in range(2):
        xb_h.append(p.op("gpsimd", lambda e, qq=qq: e.dma_start(
            out=x_bf[:, qq * 8192:(qq + 1) * 8192],
            in_=D.x[:, qq * 8192:(qq + 1) * 8192]),
            inc=xb_sems[qq], dma=True))
    xb_q = [xb_h[0], xb_h[0], xb_h[1], xb_h[1]]

    lnp_done, _, lnp_t = _emit_ln(p, sm, xpre_bf, xnp, sq, st, ab, A.cst,
                           A.ones_f, A.ones1_f, ps_r[:], ps_w0[:], ps_w1[:],
                           eps_col=1, x_ready=[mtok] + entry,
                           x_ready_t=[xp_q[T // 4] for T in range(NT)],
                           ones_b=A.ones_b, x_bf16=True)
    ln_done, _, ln_t = _emit_ln(p, sm, x_bf, xn, sq2, st2, ab2, A.cst,
                          A.ones_f, A.ones1_f, ps_r[:], ps_w0[:], ps_w1[:],
                          eps_col=1, x_ready=[mtok],
                          x_ready_t=[xb_q[T // 4] for T in range(NT)],
                          ps_free=list(p.ln_ps_done), ones_b=A.ones_b,
                          x_bf16=True)

    a1_free = [ln_done, lnp_done, (s, s.n), (t, t.n)]

    wload = [None, None]
    wuse = [None, None]
    head_end = [None, None]
    ev = {k: [None, None] for k in ("q", "k", "v", "kp", "vp")}
    pw_free = [lnp_done, lnp_done]
    tr_free = [None, None]
    sc_free = None
    at_free = None
    pr_free = ln_done
    ktc_r = [None, None]
    vtc_r = [None, None]
    kpp_r = [None, None]
    vpp_r = [None, None]
    qp_r = [None, None]
    pt_r = [None, None]
    Sb_r = None
    tS = None
    tSb = None
    attn_st = [None, None]
    w0_extra = [ln_done, (s, s.n), (t, t.n)] + entry

    for h in range(NH):
        sl = h % 2
        base = sl * 3 * DM
        lt = None
        for j, ot in enumerate((h, 16 + h, 32 + h)):
            lt = p.op("gpsimd", lambda e, j=j, ot=ot, base=base: e.dma_start(
                out=wsl[:, base + j * DM: base + (j + 1) * DM], in_=D.wqkv[ot]),
                waits=([wuse[sl]] if wuse[sl] else list(w0_extra))
                      if j == 0 else [],
                inc=g01[sl], dma=True)
        wload[sl] = lt

        ops = [("kp", kph, 1, xnp, 16 + h, lnp_t),
               ("vp", vph, 2, xnp, 32 + h, lnp_t),
               ("q", qh, 0, xn, h, ln_t),
               ("k", kh, 1, xn, 16 + h, ln_t),
               ("v", vh, 2, xn, 32 + h, ln_t)]
        for i, (nm, dstb, wj, srcb, bcol, stok_t) in enumerate(ops):
            ps = (ps_w0, ps_w1)[i % 2]
            pw = [wload[sl], pw_free[i % 2]]
            mm = None
            for hf in range(2):
                for kt in range(NT):
                    mm = p.op("tensor", lambda e, ps=ps, hf=hf, kt=kt,
                              base=base, wj=wj, srcb=srcb: e.matmul(
                        ps[:, hf * HFW:(hf + 1) * HFW],
                        wsl[:, base + wj * DM + kt * P:
                             base + wj * DM + (kt + 1) * P],
                        srcb[:, kt * TOK + hf * HFW:
                             kt * TOK + (hf + 1) * HFW],
                        start=(kt == 0), stop=(kt == NT - 1)),
                        waits=pw + [stok_t[kt]],
                        inc=(t if (hf == 1 and kt == NT - 1)
                             else None))
            wuse[sl] = mm
            evw = [mm, ev[nm][sl], tok_bq]
            if head_end[sl] is not None:
                evw += list(head_end[sl])
            etok = p.op("scalar", lambda e, ps=ps, dstb=dstb, sl=sl,
                        bcol=bcol: e.activation(
                dstb[:, sl * TOK:(sl + 1) * TOK], ps[:, 0:1024], AF.Identity,
                bias=bq[:, bcol:bcol + 1]), waits=evw, inc=s)
            ev[nm][sl] = etok
            pw_free[i % 2] = etok

        sS = None
        for cp in range(8):
            csl = cp % 2
            t1 = p.op("tensor", lambda e, cp=cp, sl=sl: e.transpose(
                ps_tr[:, 0:P],
                kph[:, sl * TOK + cp * P: sl * TOK + (cp + 1) * P], idn[:]),
                waits=[ev["kp"][sl], tr_free[0], tok_id], inc=t)
            e1 = p.op("scalar", lambda e, cp=cp, csl=csl, h=h: e.activation(
                kpp[:, csl * P:(csl + 1) * P], ps_tr[:, 0:P], AF.Copy,
                scale=pk[:, cp * 16 + h:cp * 16 + h + 1]),
                waits=[t1, tok_pk, kpp_r[csl]], inc=s)
            tr_free[0] = e1
            t2 = p.op("tensor", lambda e, cp=cp, sl=sl: e.transpose(
                ps_tr2[:, 0:P],
                vph[:, sl * TOK + cp * P: sl * TOK + (cp + 1) * P], idn[:]),
                waits=[ev["vp"][sl], tr_free[1], tok_id], inc=t)
            e2 = p.op("scalar", lambda e, csl=csl: e.copy(
                vpp[:, csl * P:(csl + 1) * P], ps_tr2[:, 0:P]),
                waits=[t2, vpp_r[csl]], inc=s)
            tr_free[1] = e2
            sS = p.op("tensor", lambda e, cp=cp, csl=csl: e.matmul(
                ps_r[:, 0:P], kpp[:, csl * P:(csl + 1) * P],
                vpp[:, csl * P:(csl + 1) * P],
                start=(cp == 0), stop=(cp == 7)),
                waits=[e1, e2] + ([pr_free] if cp == 0 else []), inc=t)
            kpp_r[csl] = sS
            vpp_r[csl] = sS
        tS = p.op("vector", lambda e: e.tensor_copy(S[:], ps_r[:, 0:P]),
                  waits=[sS], inc=v)
        tSb = p.op("vector", lambda e: e.tensor_copy(Sb[:], S[:]),
                   waits=[tS, Sb_r], inc=v)
        pr_free = tS

        lamch = float(lam_c_vals[h])
        for c in range(NCH):
            csl = c % 2
            mm_sc = p.op("tensor", lambda e, c=c, sl=sl: e.matmul(
                ps_w0[:, 0:P],
                kh[:, sl * TOK + c * P: sl * TOK + (c + 1) * P],
                qh[:, sl * TOK + c * P: sl * TOK + (c + 1) * P],
                start=True, stop=True),
                waits=[ev["k"][sl], ev["q"][sl], sc_free] +
                      ([pw_free[0]] if c == 0 else []), inc=t)
            tok_pt = p.op("vector", lambda e, csl=csl, h=h: e.tensor_tensor(
                pt[:, csl * P:(csl + 1) * P], ps_w0[:, 0:P],
                msk[:, h * P:(h + 1) * P], ALU.mult),
                waits=[mm_sc, tok_msk, pt_r[csl]], inc=v)
            sc_free = tok_pt
            t1 = p.op("tensor", lambda e, c=c, sl=sl: e.transpose(
                ps_tr[:, 0:P],
                kh[:, sl * TOK + c * P: sl * TOK + (c + 1) * P], idn[:]),
                waits=[ev["k"][sl], tr_free[0]], inc=t)
            e1 = p.op("scalar", lambda e, csl=csl, h=h: e.activation(
                ktc[:, csl * P:(csl + 1) * P], ps_tr[:, 0:P], AF.Copy,
                scale=dk[:, h:h + 1]), waits=[t1, tok_dk, ktc_r[csl]], inc=s)
            tr_free[0] = e1
            t2 = p.op("tensor", lambda e, c=c, sl=sl: e.transpose(
                ps_tr2[:, 0:P],
                vh[:, sl * TOK + c * P: sl * TOK + (c + 1) * P], idn[:]),
                waits=[ev["v"][sl], tr_free[1]], inc=t)
            e2 = p.op("scalar", lambda e, csl=csl: e.copy(
                vtc[:, csl * P:(csl + 1) * P], ps_tr2[:, 0:P]),
                waits=[t2, vtc_r[csl]], inc=s)
            tr_free[1] = e2
            tok_qp = p.op("vector", lambda e, c=c, sl=sl, csl=csl, h=h:
                          e.tensor_tensor(
                qp[:, csl * P:(csl + 1) * P],
                qh[:, sl * TOK + c * P: sl * TOK + (c + 1) * P],
                dqb[:, h * P:(h + 1) * P], ALU.mult),
                waits=[ev["q"][sl], tok_dqb, qp_r[csl]], inc=v)
            p.op("tensor", lambda e, csl=csl: e.matmul(
                ps_w1[:, 0:P], vtc[:, csl * P:(csl + 1) * P],
                pt[:, csl * P:(csl + 1) * P], start=True, stop=False),
                waits=[tok_pt, e2, at_free] +
                      ([pw_free[1]] if c == 0 else []))
            mm_at = p.op("tensor", lambda e, csl=csl: e.matmul(
                ps_w1[:, 0:P], Sb[:], qp[:, csl * P:(csl + 1) * P],
                start=False, stop=True), waits=[tok_qp, tSb], inc=t)
            pt_r[csl] = mm_at
            qp_r[csl] = mm_at
            Sb_r = mm_at
            at_free = p.op("scalar", lambda e, sl=sl, c=c: e.copy(
                attn[:, sl * TOK + c * P: sl * TOK + (c + 1) * P],
                ps_w1[:, 0:P]), waits=[mm_at, attn_st[sl]], inc=s)
            mm_s = p.op("tensor", lambda e, csl=csl: e.matmul(
                ps_r[:, 0:P], ktc[:, csl * P:(csl + 1) * P],
                vtc[:, csl * P:(csl + 1) * P], start=True, stop=True),
                waits=[e1, e2, pr_free], inc=t)
            ktc_r[csl] = mm_s
            vtc_r[csl] = mm_s
            tS = p.op("vector", lambda e, lamch=lamch: e.scalar_tensor_tensor(
                S[:], S[:], lamch, ps_r[:, 0:P], ALU.mult, ALU.add),
                waits=[mm_s], inc=v)
            pr_free = tS
            if c < NCH - 1:
                tSb = p.op("vector", lambda e: e.tensor_copy(Sb[:], S[:]),
                           waits=[tS, Sb_r], inc=v)
        head_end[sl] = [(t, t.n), (v, v.n)]
        pw_free = [sc_free, at_free]
        attn_st[sl] = p.op("sync", lambda e, h=h, sl=sl: e.dma_start(
            out=D.attn[:, h * TOK:(h + 1) * TOK],
            in_=attn[:, sl * TOK:(sl + 1) * TOK]),
            waits=[at_free], inc=d01[sl], dma=True)

    tok_xnst = p.op("sync", lambda e: e.dma_start(out=D.xn, in_=xn),
                    waits=[ln_done], inc=d, dma=True)
    return {"attn_st": [attn_st[0], attn_st[1]], "xn_st": tok_xnst,
            "a1_free": a1_free}


def _emit_s2(p, sm, A, D, entry, mtok, hook=None, a1_free=None,
             hook_ot=None):
    t, v, s, d, g = (sm[k] for k in ("t", "v", "s", "d", "g"))
    g01 = (sm["g0"], sm["g1"])

    x = A.A1[:]
    xn = A.A2[:]
    big = A.A3[:]
    h1g = A.A4[:, 0:8192]
    wsl = A.A4[:, 8192:12288]
    st = A.A4[0:1, 0:8192].bitcast(F32)
    sq = A.A5[:, 0:4096].bitcast(F32)
    ab = A.A5[:, 4096:8192].bitcast(F32)
    bo, p1b, p2b = A.bo, A.p1b, A.p2b
    ps_w0, ps_w1, ps_r = A.ps_w0, A.ps_w1, A.ps_r

    tok_xl = p.op("sync", lambda e: e.dma_start(out=x, in_=D.x),
                  waits=(a1_free if a1_free else entry), inc=sm["d2"],
                  dma=True)
    p.op("sync", lambda e: e.dma_start(out=xn, in_=D.xn), waits=entry,
         inc=d, dma=True)
    p.op("sync", lambda e: e.dma_start(out=big, in_=D.attn), inc=d, dma=True)
    cl = None
    for dst, srcd in ((bo[:], D.bo), (p1b[:], D.p1b), (p2b[:], D.p2b)):
        cl = p.op("sync", lambda e, dst=dst, srcd=srcd:
                  e.dma_start(out=dst, in_=srcd), inc=d, dma=True)
    tok_xn = tok_at = cl
    tok_x = tok_xl

    wl = [None, None]
    wu = [None, None]
    pw = [None, None]
    xadd = None
    for ot in range(NT):
        sl = ot % 2
        wl[sl] = p.op("gpsimd", lambda e, ot=ot, sl=sl: e.dma_start(
            out=wsl[:, sl * DM:(sl + 1) * DM], in_=D.wout[ot]),
            waits=[wu[sl]] if wu[sl] else entry, inc=g01[sl], dma=True)
        ps = (ps_w0, ps_w1)[sl]
        mm = None
        for hf in range(2):
            for kt in range(NT):
                mm = p.op("tensor", lambda e, ps=ps, sl=sl, hf=hf, kt=kt:
                          e.matmul(
                    ps[:, hf * HFW:(hf + 1) * HFW],
                    wsl[:, sl * DM + kt * P: sl * DM + (kt + 1) * P],
                    big[:, kt * TOK + hf * HFW: kt * TOK + (hf + 1) * HFW],
                    start=(kt == 0), stop=(kt == NT - 1)),
                    waits=[wl[sl], tok_at, pw[sl]] + (entry if ot < 2 else []),
                    inc=(t if (hf == 1 and kt == NT - 1) else None))
        wu[sl] = mm
        for hf in range(2):
            xadd = p.op("vector", lambda e, ps=ps, ot=ot, hf=hf:
                        e.scalar_tensor_tensor(
                x[:, ot * TOK + hf * HFW: ot * TOK + (hf + 1) * HFW],
                ps[:, hf * HFW:(hf + 1) * HFW], bo[:, ot:ot + 1],
                x[:, ot * TOK + hf * HFW: ot * TOK + (hf + 1) * HFW],
                ALU.add, ALU.add), waits=[mm, tok_x, cl] + entry, inc=v)
        pw[sl] = xadd
    outproj_done = xadd

    p1l = [None, None]
    p1u = [None, None]
    gelu_tok = [None] * 8
    p2sl = A.A4[:, 12288:14336]
    p2l = [None, None]
    p2u = [None, None]
    ps_free = {0: outproj_done, 1: outproj_done}
    ffn_add = outproj_done
    for gb in range(8):
        for kt in range(8):
            sl = kt % 2
            ht = gb * 8 + kt
            p1l[sl] = p.op("gpsimd", lambda e, ht=ht, sl=sl: e.dma_start(
                out=wsl[:, sl * DM:(sl + 1) * DM], in_=D.p1[ht]),
                waits=[p1u[sl] if p1u[sl] else wu[sl]], inc=g01[sl],
                dma=True)
            mm = None
            for hf in range(2):
                for ik in range(NT):
                    mm = p.op("tensor", lambda e, sl=sl, hf=hf, ik=ik:
                              e.matmul(
                        ps_w0[:, hf * HFW:(hf + 1) * HFW],
                        wsl[:, sl * DM + ik * P: sl * DM + (ik + 1) * P],
                        xn[:, ik * TOK + hf * HFW: ik * TOK + (hf + 1) * HFW],
                        start=(ik == 0), stop=(ik == NT - 1)),
                        waits=[p1l[sl], tok_xn, outproj_done,
                               gelu_tok[(kt - 1) % 8]],
                        inc=(t if (hf == 1 and ik == NT - 1) else None))
            p1u[sl] = mm
            gelu_tok[kt] = p.op("scalar", lambda e, kt=kt, ht=ht:
                                e.activation(
                h1g[:, kt * TOK:(kt + 1) * TOK], ps_w0[:, 0:1024], AF.Gelu,
                bias=p1b[:, ht:ht + 1]), waits=[mm, cl], inc=s)
        for ot in range(NT):
            psel = ot % 2
            sl2 = ot % 2
            p2l[sl2] = p.op("gpsimd", lambda e, gb=gb, ot=ot, sl2=sl2:
                            e.dma_start(
                out=p2sl[:, sl2 * 1024:(sl2 + 1) * 1024],
                in_=D.p2g[gb, :, ot * 1024:(ot + 1) * 1024]),
                waits=([p2u[sl2]] if p2u[sl2] else
                       ([(t, t.n)] + entry if gb == 0 and ot < 2 else [])),
                inc=sm["g2"] if sl2 == 0 else sm["g3"], dma=True)
            ps = (ps_w1, ps_r)[psel]
            mm = None
            for hf in range(2):
                for kt in range(8):
                    mm = p.op("tensor", lambda e, ps=ps, sl2=sl2, hf=hf, kt=kt:
                              e.matmul(
                        ps[:, hf * HFW:(hf + 1) * HFW],
                        p2sl[:, sl2 * 1024 + kt * P: sl2 * 1024 + (kt + 1) * P],
                        h1g[:, kt * TOK + hf * HFW: kt * TOK + (hf + 1) * HFW],
                        start=(kt == 0), stop=(kt == 7)),
                        waits=[p2l[sl2], gelu_tok[kt], ps_free[psel]],
                        inc=(t if (hf == 1 and kt == 7) else None))
            p2u[sl2] = mm
            for hf in range(2):
                if gb == 0:
                    ffn_add = p.op("vector", lambda e, ps=ps, ot=ot, hf=hf:
                                   e.scalar_tensor_tensor(
                        x[:, ot * TOK + hf * HFW: ot * TOK + (hf + 1) * HFW],
                        ps[:, hf * HFW:(hf + 1) * HFW], p2b[:, ot:ot + 1],
                        x[:, ot * TOK + hf * HFW: ot * TOK + (hf + 1) * HFW],
                        ALU.add, ALU.add), waits=[mm, outproj_done, cl],
                        inc=v)
                else:
                    ffn_add = p.op("vector", lambda e, ps=ps, ot=ot, hf=hf:
                                   e.tensor_tensor(
                        x[:, ot * TOK + hf * HFW: ot * TOK + (hf + 1) * HFW],
                        x[:, ot * TOK + hf * HFW: ot * TOK + (hf + 1) * HFW],
                        ps[:, hf * HFW:(hf + 1) * HFW], ALU.add),
                        waits=[mm], inc=v)
            ps_free[psel] = ffn_add
            if hook_ot is not None:
                hook_ot(gb, ot)
        if hook is not None:
            hook(gb)

    st1 = p.op("sync", lambda e: e.dma_start(out=D.x1, in_=x),
               waits=[ffn_add], inc=d, dma=True)
    ln1_done, _, ln1_t = _emit_ln(p, sm, x, xn, sq, st, ab, A.cst, A.ones_f,
                           A.ones1_f, ps_r[:], ps_w0[:], ps_w1[:], eps_col=1,
                           x_ready=[ffn_add, mtok],
                           ps_free=[(t, t.n)], ones_b=A.ones_b)
    st2 = p.op("sync", lambda e: e.dma_start(out=D.xn1, in_=xn),
               waits=[ln1_done], inc=d, dma=True)
    a1f2 = [ln1_done, (s, s.n), (t, t.n), st2]
    return {"x1_st": st1, "xn1_st": st2, "ln1_done": ln1_done,
            "ln1_t": ln1_t, "a1f2": a1f2}



class _DeqEarly:

    LOOKAHEAD = 2

    def __init__(self, p, sm, A, D, entry):
        self.p, self.sm, self.A, self.D = p, sm, A, D
        self.entry = list(entry)
        self.R = None
        self.guard = list(entry)
        self.ld_region = {}
        self.units = []
        NB = [1024] * 5 + [512]
        for Jp in range(6):
            W = NB[Jp]
            for sect in range(2):
                sec = Jp * 2 + sect
                ob = Jp * 1024 if sect == 0 else II + Jp * 1024
                for J in range(2):
                    for pair in range(4):
                        self.units.append(("gu", sec, J, pair, ob, W))
        for Jp in range(6):
            for pair in range(4):
                for jh in range(2):
                    self.units.append(("dn", Jp, jh, pair, 0, 1024))
        self.idx = 0
        self.nload = 0
        self.t6_chain = [None, None]
        self.st_chain = [None, None]
        self.store_tok = {}
        self.ld_tok = {}

    def set_region(self, which, guard):
        R = NS()
        if which == "a1":
            a1 = self.A.A1
            R.stg = [a1[:, 6144:8192].bitcast(BF16),
                     a1[:, 8192:10240].bitcast(BF16)]
            R.pk2 = a1[:, 10240:12288].bitcast(I32)
            R.sc2 = a1[:, 12288:14336].bitcast(F16)
            R.dtmp = a1[:, 14336:16384].bitcast(BF16)
            R.dsg = a1[:, 14336:16384].bitcast(I32)
        else:
            a3 = self.A.A3
            R.stg = [a3[:, 0:2048], a3[:, 2048:4096]]
            R.pk2 = a3[:, 4096:8192].bitcast(I32)
            R.sc2 = a3[:, 8192:12288].bitcast(F16)
            R.dtmp = a3[:, 12288:14336]
            R.dsg = a3[:, 14336:16384].bitcast(I32)
        self.R = R
        self.guard = list(guard)

    def take(self, k):
        for _ in range(k):
            if self.idx >= len(self.units):
                return
            while self.nload < min(self.idx + self.LOOKAHEAD,
                                   len(self.units)):
                self._loads(self.nload)
                self.nload += 1
            self._compute(self.units[self.idx])
            self.idx += 1

    def finish(self):
        self.take(len(self.units))

    def drain(self):
        while self.idx < self.nload:
            self._compute(self.units[self.idx])
            self.idx += 1

    def _views(self, u, q, Rg):
        kind, sec, J, pair, ob, W = u
        if kind == "gu":
            R = P
            pk = Rg.pk2[:, q * 1024: q * 1024 + W]
            sc = Rg.sc2[:, q * 2048: q * 2048 + 2 * W]
            srcp = self.D.gupk[J * P:(J + 1) * P, ob: ob + W]
            srcs = self.D.gusc[J * P:(J + 1) * P, 2 * ob: 2 * ob + 2 * W]
            dst = self.D.gw[sec][:, (J * 4 + pair) * 2048:
                                 (J * 4 + pair) * 2048 + 2 * W]
        else:
            Jp, jh = sec, J
            R = 128 if Jp < 5 else 64
            pk = Rg.pk2[0:R, q * 1024:(q + 1) * 1024]
            sc = Rg.sc2[0:R, q * 2048:(q + 1) * 2048]
            srcp = self.D.dnpk[Jp * P: Jp * P + R, jh * 1024:(jh + 1) * 1024]
            srcs = self.D.dnsc[Jp * P: Jp * P + R, jh * 2048:(jh + 1) * 2048]
            dst = self.D.dw[Jp][0:R, pair * 4096 + jh * 2048:
                                pair * 4096 + jh * 2048 + 2048]
        return pk, sc, srcp, srcs, dst, R

    def _loads(self, uidx):
        p, sm = self.p, self.sm
        u = self.units[uidx]
        q = uidx % 2
        dql = sm["dql0"] if q == 0 else sm["dql1"]
        self.ld_region[uidx] = self.R
        pk, sc, srcp, srcs, dst, R = self._views(u, q, self.R)
        lw = [self.t6_chain[q]] + self.guard
        p.op("sync", lambda e, pk=pk, srcp=srcp: e.dma_start(
            out=pk, in_=srcp), waits=lw, inc=dql, dma=True)
        l2 = p.op("sync", lambda e, sc=sc, srcs=srcs: e.dma_start(
            out=sc, in_=srcs), inc=dql, dma=True)
        self.ld_tok[uidx] = l2

    def _compute(self, u):
        p, sm = self.p, self.sm
        v = sm["v"]
        kind, sec, J, pair, ob, W = u
        q = self.idx % 2
        dqs = sm["dqs0"] if q == 0 else sm["dqs1"]
        Rg = self.ld_region[self.idx]
        pk, sc, srcp, srcs, dst, R = self._views(u, q, Rg)
        tmp = Rg.dtmp[0:R, 0:2 * W]
        sgp = Rg.dsg[0:R, 0:W]
        stg = Rg.stg[q][0:R, 0:2 * W]
        t7 = _deq_ops(p, v, [self.ld_tok[self.idx], self.st_chain[q]] +
                      self.guard, pk, sc, stg, tmp, sgp, pair)
        self.t6_chain[q] = (t7[0], t7[1] - 1)
        stok = p.op("sync", lambda e, dst=dst, stg=stg: e.dma_start(
            out=dst, in_=stg), waits=[t7], inc=dqs, dma=True)
        self.st_chain[q] = stok
        key = ("gu", sec, J * 4 + pair) if kind == "gu" else ("dn", sec, pair)
        self.store_tok.setdefault(key, []).append(stok)


def _emit_s3(p, sm, A, D, entry, mtok, hook=None, xn1_t=None,
             entry_scr=None):
    t, v, s, d, g = (sm[k] for k in ("t", "v", "s", "d", "g"))
    g01 = (sm["g0"], sm["g1"])
    g23 = (sm["g2"], sm["g3"])
    d01 = (sm["d0"], sm["d1"])

    xn1 = A.A2[:]
    qT = A.A3[:]
    aU = A.A4[:]
    ksl = A.A5[:, 0:8192]
    vsl = A.A5[:, 8192:16384]
    wsl = A.A1[:, 0:2048].bitcast(BF16)
    ptb = A.A1[:, 2048:3072].bitcast(BF16)
    rb = A.A1[:, 3072:4096]
    rr = A.A1[0:1, 4096:5120]
    wsl2 = A.A5[:, 0:4096]
    stg = A.A5[:, 4096:6144]
    cst, ones_b, ones1_f = A.cst, A.ones_b, A.ones1_f
    ps_w0, ps_w1 = A.ps_w0, A.ps_w1
    ps_at = A.ps_r
    ps_sm = [A.ps_tr[:].bitcast(F32), A.ps_tr2[:].bitcast(F32)]

    if xn1_t is None:
        tok_xn = p.op("sync", lambda e: e.dma_start(out=xn1, in_=D.xn1),
                      waits=entry, inc=d, dma=True)
    else:
        tok_xn = None

    wl = [None, None]
    wu = [None, None]
    evt = [None, None]
    for ot in range(NT):
        sl = ot % 2
        wl[sl] = p.op("gpsimd", lambda e, ot=ot, sl=sl: e.dma_start(
            out=wsl[:, sl * DM:(sl + 1) * DM], in_=D.wq[ot]),
            waits=[wu[sl]] if wu[sl] else entry, inc=g01[sl], dma=True)
        ps = (ps_w0, ps_w1)[sl]
        mm = None
        for hf in range(2):
            for kt in range(NT):
                mm = p.op("tensor", lambda e, ps=ps, sl=sl, hf=hf, kt=kt:
                          e.matmul(
                    ps[:, hf * HFW:(hf + 1) * HFW],
                    wsl[:, sl * DM + kt * P: sl * DM + (kt + 1) * P],
                    xn1[:, kt * TOK + hf * HFW: kt * TOK + (hf + 1) * HFW],
                    start=(kt == 0), stop=(kt == NT - 1)),
                    waits=[wl[sl],
                           xn1_t[kt] if xn1_t else tok_xn, evt[sl]] +
                          (entry if ot < 2 else []),
                    inc=(t if (hf == 1 and kt == NT - 1) else None))
        wu[sl] = mm
        evt[sl] = p.op("scalar", lambda e, ps=ps, ot=ot: e.copy(
            qT[:, ot * TOK:(ot + 1) * TOK], ps[:, 0:1024]),
            waits=[mm] + ((entry_scr if entry_scr is not None else entry)
                          if ot < 2 else []), inc=s)
    q_done = [evt[0], evt[1]]

    kl = [None, None]
    ku = [None, None]
    aU_tok = None
    prev_aU = None
    expf = {0: evt[0], 1: evt[1]}
    smf = None
    rbe = None
    for h in range(NH):
        sl = h % 2
        p.op("gpsimd", lambda e, h=h, sl=sl: e.dma_start(
            out=ksl[:, sl * KV:(sl + 1) * KV], in_=D.kvkT[h]),
            waits=[ku[sl]] if ku[sl] else entry, inc=g23[sl], dma=True)
        vl = p.op("gpsimd", lambda e, h=h, sl=sl: e.dma_start(
            out=vsl[:, sl * KV:(sl + 1) * KV].rearrange(
                "p (kc dd) -> p kc dd", dd=P),
            in_=D.kvv[h].rearrange("(kc p) dd -> p kc dd", p=P)),
            inc=g23[sl], dma=True)
        kl[sl] = vl
        mm_last = None
        extok = [None, None]
        avtok = [None, None]

        def _emit_av(kc):
            nonlocal mm_last
            ksl2 = kc % 2
            for hf in range(2):
                mm_last = p.op("tensor", lambda e, sl=sl, kc=kc, ksl2=ksl2,
                               hf=hf: e.matmul(
                    ps_at[:, hf * HFW:(hf + 1) * HFW],
                    vsl[:, sl * KV + kc * P: sl * KV + (kc + 1) * P],
                    ptb[:, ksl2 * 1024 + hf * HFW:
                        ksl2 * 1024 + (hf + 1) * HFW],
                    start=(kc == 0), stop=(kc == 31)),
                    waits=[extok[ksl2], kl[sl]] +
                          ([prev_aU] if kc == 0 else []),
                    inc=(t if (hf == 1 and kc == 31) else None))
            for hf in range(2):
                p.op("tensor", lambda e, ksl2=ksl2, hf=hf, kc=kc: e.matmul(
                    ps_sm[hf][0:1, 0:HFW], ones_b[:, 0:1],
                    ptb[:, ksl2 * 1024 + hf * HFW:
                        ksl2 * 1024 + (hf + 1) * HFW],
                    start=(kc == 0), stop=(kc == 31)),
                    waits=[mtok] + ([smf] if kc == 0 else []),
                    inc=(t if (hf == 1 and kc == 31) else None))
            avtok[ksl2] = mm_last

        for kc in range(32):
            ksl2 = kc % 2
            ps = (ps_w0, ps_w1)[ksl2]
            mm = None
            for hf in range(2):
                mm = p.op("tensor", lambda e, ps=ps, h=h, sl=sl, kc=kc,
                          hf=hf: e.matmul(
                    ps[:, hf * HFW:(hf + 1) * HFW],
                    ksl[:, sl * KV + kc * P: sl * KV + (kc + 1) * P],
                    qT[:, h * TOK + hf * HFW: h * TOK + (hf + 1) * HFW],
                    start=True, stop=True),
                    waits=[kl[sl], expf[ksl2]] + q_done,
                    inc=(t if hf == 1 else None))
            ex = p.op("scalar", lambda e, ps=ps, ksl2=ksl2: e.activation(
                ptb[:, ksl2 * 1024:(ksl2 + 1) * 1024], ps[:, 0:1024],
                AF.Exp, bias=cst[:, 0:1], scale=SCL1),
                waits=[mm, mtok, avtok[ksl2]], inc=s)
            expf[ksl2] = ex
            extok[ksl2] = ex
            if kc >= 1:
                _emit_av(kc - 1)
        _emit_av(31)
        ku[sl] = mm_last
        sum_tok = (t, t.n)
        rec = None
        for hf in range(2):
            rec = p.op("vector", lambda e, hf=hf: e.reciprocal(
                rr[0:1, hf * HFW:(hf + 1) * HFW], ps_sm[hf][0:1, 0:HFW]),
                waits=[sum_tok] + (entry if h == 0 else []), inc=v)
        smf = rec
        bc = None
        for hf in range(2):
            bc = p.op("tensor", lambda e, hf=hf: e.matmul(
                ps_w0[:, hf * HFW:(hf + 1) * HFW], ones1_f[0:1, :],
                rr[0:1, hf * HFW:(hf + 1) * HFW], start=True, stop=True),
                waits=[rec, expf[0], mtok], inc=t)
        rbe = p.op("scalar", lambda e: e.copy(rb[:], ps_w0[:, 0:1024]),
                   waits=[bc, prev_aU], inc=s)
        expf[0] = rbe
        aU_tok = p.op("vector", lambda e, h=h: e.tensor_tensor(
            aU[:, h * TOK:(h + 1) * TOK], ps_at[:, 0:1024], rb[:], ALU.mult),
            waits=[rbe, mm_last] + (entry if h == 0 else []), inc=v)
        prev_aU = aU_tok
        if hook is not None:
            hook(h)

    pre_wo = [(t, t.n), (sm["v"], sm["v"].n), (s, s.n)]
    wl2 = [None, None]
    wu2 = [None, None]
    ev2 = [rbe, None]
    at_st = [None, None]
    for ot in range(NT):
        sl = ot % 2
        wl2[sl] = p.op("gpsimd", lambda e, ot=ot, sl=sl: e.dma_start(
            out=wsl2[:, sl * DM:(sl + 1) * DM], in_=D.wo[ot]),
            waits=[wu2[sl] if wu2[sl] else ku[sl]], inc=g01[sl], dma=True)
        ps = (ps_w0, ps_w1)[sl]
        mm = None
        for hf in range(2):
            for kt in range(NT):
                mm = p.op("tensor", lambda e, ps=ps, sl=sl, hf=hf, kt=kt:
                          e.matmul(
                    ps[:, hf * HFW:(hf + 1) * HFW],
                    wsl2[:, sl * DM + kt * P: sl * DM + (kt + 1) * P],
                    aU[:, kt * TOK + hf * HFW: kt * TOK + (hf + 1) * HFW],
                    start=(kt == 0), stop=(kt == NT - 1)),
                    waits=[wl2[sl], aU_tok, ev2[sl]],
                    inc=(t if (hf == 1 and kt == NT - 1) else None))
        wu2[sl] = mm
        ev2[sl] = p.op("scalar", lambda e, ps=ps, sl=sl: e.copy(
            stg[:, sl * 1024:(sl + 1) * 1024], ps[:, 0:1024]),
            waits=[mm, at_st[sl]], inc=s)
        at_st[sl] = p.op("sync", lambda e, ot=ot, sl=sl: e.dma_start(
            out=D.at1[:, ot * TOK:(ot + 1) * TOK],
            in_=stg[:, sl * 1024:(sl + 1) * 1024]),
            waits=[ev2[sl]], inc=d01[sl], dma=True)
    return {"at1_st": [at_st[0], at_st[1]], "pre_wo": pre_wo,
            "wo_end": [(t, t.n), (s, s.n)]}


def _deq_ops(p, v, waits, src_i32, src_sc, dst_bf, tmp_bf, sg_i32, pair,
             permute=False):
    mg = dst_bf.bitcast(I32)
    mg1 = mg.rearrange("p (b a) -> p a b", b=8) if permute else mg
    sg1 = (sg_i32.rearrange("p (b a) -> p a b", b=8) if permute else sg_i32)
    shl = 6 - 4 * pair
    if shl >= 0:
        t1 = p.op("vector", lambda e: e.tensor_scalar(
            mg1, src_i32, shl, 0x01C001C0, ALU.logical_shift_left,
            ALU.bitwise_and), waits=waits, inc=v)
    else:
        t1 = p.op("vector", lambda e: e.tensor_scalar(
            mg1, src_i32, -shl, 0x01C001C0, ALU.logical_shift_right,
            ALU.bitwise_and), waits=waits, inc=v)
    t2 = p.op("vector", lambda e: e.tensor_scalar(
        mg, mg, 0x3F003F00, None, ALU.add), waits=[t1], inc=v)
    t3 = p.op("vector", lambda e: e.tensor_tensor(
        dst_bf, dst_bf, src_sc, ALU.mult), waits=[t2], inc=v)
    t4 = p.op("vector", lambda e: e.scalar_tensor_tensor(
        tmp_bf, dst_bf, 2.0, src_sc, ALU.mult, ALU.subtract), waits=[t3], inc=v)
    t5 = p.op("vector", lambda e: e.tensor_tensor(
        dst_bf, dst_bf, tmp_bf, ALU.min), waits=[t4], inc=v)
    t6 = p.op("vector", lambda e: e.tensor_scalar(
        sg1, src_i32, 12 - 4 * pair, -2147450880,
        ALU.logical_shift_left, ALU.bitwise_and), waits=[t5], inc=v)
    t7 = p.op("vector", lambda e: e.tensor_tensor(
        mg, mg, sg_i32, ALU.bitwise_or), waits=[t6], inc=v)
    return t7


def _emit_s4(p, sm, A, D, entry, mtok, dq, have_xn1=False, pre=None):
    t, v, s, d, g = (sm[k] for k in ("t", "v", "s", "d", "g"))
    if pre is None:
        pre = {}
    pre_wo = pre.get("pre_wo", entry)
    wo_end = pre.get("wo_end", entry)
    at1_st = pre.get("at1_st", [])
    x1_st = pre.get("x1_st", None)
    dqfin = [(sm[k], sm[k].n) for k in ("dqs0", "dqs1", "dql0", "dql1")]

    y = A.A1[:]
    xn1 = A.A2[:]
    bufs = (A.A3[:], A.A5[:])
    hcur = A.A4[:, 0:8192]
    sg = A.A4[:, 8192:16384]
    sq = A.A3[:, 0:4096].bitcast(F32)
    ab = A.A5[:, 4096:8192].bitcast(F32)
    st = A.A5[0:1, 8192:16384].bitcast(F32)
    x1t = A.A2[:, 0:4096].bitcast(F32)
    ps_w0, ps_w1, ps_r = A.ps_w0, A.ps_w1, A.ps_r

    NB = [1024] * 5 + [512]

    dqfin2 = dqfin
    tok_xn = None
    if not have_xn1:
        tok_xn = p.op("sync", lambda e: e.dma_start(out=xn1, in_=D.xn1),
                      waits=entry, inc=d, dma=True)
    buf_free = [list(pre_wo), list(wo_end) + list(at1_st)]
    sec_i = 0
    hfree = None
    ht_tok = [None] * 8
    sg_tok = [None] * 8
    ps_free = {0: None, 1: None}
    ydone = None
    ydone_t = {}
    xn1_free = None
    for Jp in range(6):
        W = NB[Jp]
        for sect in range(2):
            sec = Jp * 2 + sect
            bq_ = sec_i % 2
            big = bufs[bq_]
            bfw = buf_free[bq_]
            bfw = bfw if isinstance(bfw, list) else [bfw]
            gsec = g if bq_ == 0 else sm["g2"]
            ld = None
            for sb in range(8):
                ld = p.op("gpsimd", lambda e, sec=sec, sb=sb, W=W, big=big:
                          e.dma_start(
                    out=big[:, sb * 2048: sb * 2048 + 2 * W],
                    in_=D.gw[sec][:, sb * 2048: sb * 2048 + 2 * W]),
                    waits=(bfw if sb == 0 else []) +
                          dq.store_tok[("gu", sec, sb)],
                    inc=gsec, dma=True)
            mm_done = None
            nrow = W // 8
            for pp in range(8):
                psel = pp % 2
                ps = (ps_w0, ps_w1)[psel]
                mm = None
                for hf in range(2):
                    for T in range(NT):
                        J, pl = T // 8, T % 8
                        pair, half = pl % 4, pl // 4
                        off = (J * 4 + pair) * 2048 + 2 * pp + half
                        xw = [ld, tok_xn] if tok_xn else [ld]
                        if T == 0 and hf == 0:
                            xw.append(ps_free[psel])
                            if sec_i == 0:
                                xw += list(wo_end)
                        xw = [w_ for w_ in xw if w_ is not None]
                        mm = p.op("tensor", lambda e, ps=ps, off=off, big=big,
                                  nrow=nrow, T=T, hf=hf: e.matmul(
                            ps[0:nrow, hf * HFW:(hf + 1) * HFW],
                            big[:, off: off + 16 * (nrow - 1) + 1: 16],
                            xn1[:, T * TOK + hf * HFW:
                                T * TOK + (hf + 1) * HFW],
                            start=(T == 0), stop=(T == NT - 1)),
                            waits=xw,
                            inc=(t if (hf == 1 and T == NT - 1) else None))
                if sect == 0:
                    sg_tok[pp] = p.op("scalar", lambda e, ps=ps, pp=pp,
                                      nrow=nrow: e.activation(
                        sg[0:nrow, pp * TOK:(pp + 1) * TOK],
                        ps[0:nrow, 0:1024], AF.Silu),
                        waits=[mm, ht_tok[pp]], inc=s)
                    ps_free[psel] = sg_tok[pp]
                else:
                    ht_tok[pp] = p.op("vector", lambda e, ps=ps, pp=pp,
                                      nrow=nrow: e.tensor_tensor(
                        hcur[0:nrow, pp * TOK:(pp + 1) * TOK],
                        ps[0:nrow, 0:1024],
                        sg[0:nrow, pp * TOK:(pp + 1) * TOK], ALU.mult),
                        waits=[mm, sg_tok[pp], hfree], inc=v)
                    ps_free[psel] = ht_tok[pp]
                mm_done = mm
            buf_free[bq_] = [mm_done]
            if Jp == 5 and sect == 1:
                xn1_free = mm_done
            sec_i += 1

        R = 128 if Jp < 5 else 64
        bq_ = sec_i % 2
        dbig = bufs[bq_]
        bfw = buf_free[bq_]
        bfw = bfw if isinstance(bfw, list) else [bfw]
        gsec = g if bq_ == 0 else sm["g2"]
        dld = None
        for pair in range(4):
            dld = p.op("gpsimd", lambda e, Jp=Jp, pair=pair, R=R, dbig=dbig:
                       e.dma_start(
                out=dbig[0:R, pair * 4096:(pair + 1) * 4096],
                in_=D.dw[Jp][0:R, pair * 4096:(pair + 1) * 4096]),
                waits=(bfw if pair == 0 else []) +
                      dq.store_tok[("dn", Jp, pair)],
                inc=gsec, dma=True)
        for ot in range(NT):
            psel = ot % 2
            ps = (ps_w0, ps_w1)[psel]
            mm = None
            for hf in range(2):
                for pl in range(8):
                    pair, half = pl % 4, pl // 4
                    off = pair * 4096 + 2 * (ot * P) + half
                    mm = p.op("tensor", lambda e, ps=ps, off=off, pl=pl,
                              hf=hf, R=R, dbig=dbig: e.matmul(
                        ps[:, hf * HFW:(hf + 1) * HFW],
                        dbig[0:R, off: off + 2 * (P - 1) + 1: 2],
                        hcur[0:R, pl * TOK + hf * HFW:
                             pl * TOK + (hf + 1) * HFW],
                        start=(pl == 0), stop=(pl == 7)),
                        waits=[dld, ht_tok[pl],
                               ps_free[psel] if (pl == 0 and hf == 0)
                               else None],
                        inc=(t if (hf == 1 and pl == 7) else None))
            for hf in range(2):
                if Jp == 0:
                    ydone = p.op("vector", lambda e, ps=ps, ot=ot, hf=hf:
                                 e.tensor_copy(
                        y[:, ot * TOK + hf * HFW: ot * TOK + (hf + 1) * HFW],
                        ps[:, hf * HFW:(hf + 1) * HFW]),
                        waits=[mm] + (dqfin2 if ot < 2 else []) +
                              (list(pre_wo) if ot < 2 else []), inc=v)
                else:
                    ydone = p.op("vector", lambda e, ps=ps, ot=ot, hf=hf:
                                 e.tensor_tensor(
                        y[:, ot * TOK + hf * HFW: ot * TOK + (hf + 1) * HFW],
                        y[:, ot * TOK + hf * HFW: ot * TOK + (hf + 1) * HFW],
                        ps[:, hf * HFW:(hf + 1) * HFW], ALU.add),
                        waits=[mm], inc=v)
            if Jp == 5:
                ydone_t[ot] = ydone
            ps_free[psel] = ydone
        buf_free[bq_] = [(t, t.n)]
        hfree = (t, t.n)
        sec_i += 1

    rms_tok, _, _rt = _emit_ln(p, sm, y, y, sq, st, ab, A.cst, A.ones_f,
                          A.ones1_f, ps_r[:], ps_w0[:], ps_w1[:], eps_col=2,
                          x_ready=[mtok],
                          x_ready_t=[ydone_t[T] for T in range(NT)],
                          ps_free=[(t, t.n), (v, v.n)], rms=True,
                          ones_b=A.ones_b)
    x1t4 = A.A2[:, 0:8192].bitcast(F32)
    at1t = A.A2[:, 8192:12288]
    sq4 = A.A3[:, 0:8192].bitcast(F32)
    xsem = [d, sm["d0"], sm["d1"], sm["d2"]]
    asem = [sm["dql0"], sm["dql1"], sm["dqs0"], sm["dqs1"]]
    x1l = [None] * 4
    a1l = [None] * 4
    stf4 = [None] * 4
    stf = None
    x1st = pre.get("x1_st", None)

    def _tail_load(T, gate):
        sl = T % 4
        x1l[sl] = p.op("sync", lambda e, T=T, sl=sl: e.dma_start(
            out=x1t4[:, sl * 1024:(sl + 1) * 1024],
            in_=D.x1[:, T * TOK:(T + 1) * TOK]),
            waits=gate, inc=xsem[sl], dma=True)
        a1l[sl] = p.op("sync", lambda e, T=T, sl=sl: e.dma_start(
            out=at1t[:, sl * 1024:(sl + 1) * 1024],
            in_=D.at1[:, T * TOK:(T + 1) * TOK]),
            waits=gate, inc=asem[sl], dma=True)

    for T in range(4):
        _tail_load(T, [xn1_free, x1st] + list(at1_st))
    for T in range(NT):
        sl = T % 4
        t1 = p.op("vector", lambda e, T=T, sl=sl: e.tensor_tensor(
            sq4[:, sl * 1024:(sl + 1) * 1024], y[:, T * TOK:(T + 1) * TOK],
            ab[:, 0:1024], ALU.mult), waits=[rms_tok], inc=v)
        t2 = p.op("vector", lambda e, T=T, sl=sl: e.tensor_tensor(
            sq4[:, sl * 1024:(sl + 1) * 1024],
            sq4[:, sl * 1024:(sl + 1) * 1024],
            x1t4[:, sl * 1024:(sl + 1) * 1024], ALU.add),
            waits=[t1, x1l[sl]], inc=v)
        stf = p.op("vector", lambda e, T=T, sl=sl: e.tensor_tensor(
            y[:, T * TOK:(T + 1) * TOK], sq4[:, sl * 1024:(sl + 1) * 1024],
            at1t[:, sl * 1024:(sl + 1) * 1024], ALU.add),
            waits=[t2, a1l[sl]], inc=v)
        stf4[sl] = stf
        if T + 4 < NT:
            _tail_load(T + 4, [stf])
    sto = p.op("sync", lambda e: e.dma_start(out=D.xout, in_=y),
               waits=[stf], inc=d, dma=True)
    return {"out_st": sto}


_DRAM_SPECS = [
    ("dmaskT", "msk", [P, NH * P], BF16), ("dqb", "dqb", [P, NH * P], BF16),
    ("dk", "dk", [P, NH], F32), ("pk", "pk", [P, P], F32),
    ("ident", "idn", [P, P], BF16), ("bqkv", "bq", [P, 48], F32),
    ("xpi", "x", [P, NT * TOK], F32), ("xpre", "xpre", [P, NT * TOK], F32),
    ("wqkv", "wqkv", [48, P, DM], F32), ("wout", "wout", [NT, P, DM], F32),
    ("bout", "bo", [P, NT], F32), ("p1", "p1", [NFT, P, DM], F32),
    ("p1b", "p1b", [P, NFT], F32), ("p2g", "p2g", [8, P, NT * 1024], F32),
    ("p2b", "p2b", [P, NT], F32), ("wq", "wq", [NT, P, DM], F32),
    ("kvkT", "kvkT", [NH, P, KV], F32), ("kvv", "kvv", [NH, KV, P], F32),
    ("wo", "wo", [NT, P, DM], F32), ("gupk", "gupk", [256, GU], I32),
    ("gusc", "gusc", [256, 2 * GU], F16), ("dnpk", "dnpk", [704, DM], I32),
    ("dnsc", "dnsc", [704, 2 * DM], F16),
]
_HANDOFF_SPECS = [
    ("attnT", "attn", [P, NT * TOK], BF16), ("xnT", "xn", [P, NT * TOK], BF16),
    ("x1T", "x1", [P, NT * TOK], F32), ("xn1T", "xn1", [P, NT * TOK], BF16),
    ("at1T", "at1", [P, NT * TOK], BF16),
]


def _mk_dram(nc, handoff_kind, out_names=()):
    D = NS()
    for name, attr, shape, dty in _DRAM_SPECS:
        kind = "ExternalOutput" if name in out_names else "ExternalInput"
        setattr(D, attr, nc.dram_tensor(name, shape, dty, kind=kind).ap())
    for name, attr, shape, dty in _HANDOFF_SPECS:
        kind = ("ExternalOutput" if name in out_names else
                ("ExternalInput" if handoff_kind == "io" else "Internal"))
        setattr(D, attr, nc.dram_tensor(name, shape, dty, kind=kind).ap())
    D.x1p = nc.dram_tensor("x1p", [P, NT * TOK], F32, kind="Internal").ap()
    D.gw = nc.dram_tensor("gwd", [12, P, NT * TOK], BF16, kind="Internal").ap()
    D.dw = nc.dram_tensor("dwd", [6, P, NT * TOK], BF16, kind="Internal").ap()
    D.xout = nc.dram_tensor("xout", [P, NT * TOK], F32,
                            kind="ExternalOutput" if "xout" in out_names or
                            handoff_kind == "fused" else "Internal").ap()
    return D


def _build_fused():
    nc = bass.Bass("TRN2", target_bir_lowering=False)
    D = _mk_dram(nc, "fused")
    with ExitStack() as ctx:
        e_ = ctx.enter_context
        A = _alloc(nc, e_)
        blk = e_(nc.Block())
        p = Prog(nc, ctx)
        sm = _mk_sems(p)
        mtok = _emit_consts(p, sm, A)
        r1 = _emit_s1(p, sm, A, D, [], mtok)
        e1 = _stage_end(sm)
        dq = _DeqEarly(p, sm, A, D, e1)
        st2 = {"init": False}

        def _dq_init():
            if not st2["init"]:
                dq.set_region("a3", [(sm["t"], sm["t"].n),
                                     (sm["d0"], sm["d0"].n),
                                     (sm["d1"], sm["d1"].n)])
                st2["init"] = True

        def s2hook(gb):
            _dq_init()
            dq.take(4 if gb == 7 else 6)

        def s2hook_ot(gb, ot):
            _dq_init()
            if gb == 7:
                if ot in (1, 3, 5, 7, 9, 11, 13):
                    dq.take(1)
            elif ot in (2, 5, 8, 11, 14):
                dq.take(1)

        r2 = _emit_s2(p, sm, A, D, e1, mtok, hook=s2hook,
                      a1_free=r1["a1_free"], hook_ot=s2hook_ot)
        dq.drain()
        e2 = _stage_end(sm)
        st3 = {"init": False}

        def s3hook(h):
            if not st3["init"]:
                dq.set_region("a1", e2)
                st3["init"] = True
            dq.take(4)

        e2a = list(r2["a1f2"])
        e2b = e2a + [
            (sm[k], sm[k].n) for k in ("dqs0", "dqs1", "dql0", "dql1", "v")]
        r3 = _emit_s3(p, sm, A, D, e2a, mtok, hook=s3hook,
                      xn1_t=r2["ln1_t"], entry_scr=e2b)
        dq.finish()
        e3 = _stage_end(sm)
        r4 = _emit_s4(p, sm, A, D, e3, mtok, dq, have_xn1=True,
                      pre={"pre_wo": r3["pre_wo"], "wo_end": r3["wo_end"],
                           "at1_st": r3["at1_st"], "x1_st": r2["x1_st"]})
        p.op("sync", lambda e: e.nop(), waits=[r4["out_st"]])
        p.emit(blk)
    return nc


def _build_stage_n(n):
    out_names = {1: ("attnT", "xnT"), 2: ("x1T", "xn1T"), 3: ("at1T",),
                 4: ("xout",)}[n]
    nc = bass.Bass("TRN2", target_bir_lowering=False)
    D = _mk_dram(nc, "io", out_names)
    emit = {1: _emit_s1, 2: _emit_s2, 3: _emit_s3, 4: _emit_s4}[n]
    with ExitStack() as ctx:
        e_ = ctx.enter_context
        A = _alloc(nc, e_)
        blk = e_(nc.Block())
        p = Prog(nc, ctx)
        sm = _mk_sems(p)
        mtok = _emit_consts(p, sm, A)
        if n == 4:
            dq = _DeqEarly(p, sm, A, D, [])
            dq.set_region("a1", [])
            dq.finish()
            r = emit(p, sm, A, D, _stage_end(sm), mtok, dq)
        else:
            r = emit(p, sm, A, D, [], mtok)
        toks = []
        for vv in r.values():
            toks += vv if isinstance(vv, list) else [vv]
        p.op("sync", lambda e: e.nop(), waits=toks)
        p.emit(blk)
    return nc


def _build_s1():
    return _build_stage_n(1)


def _build_s2():
    return _build_stage_n(2)


def _build_s3():
    return _build_stage_n(3)


def _build_s4():
    return _build_stage_n(4)


_NC_CACHE = {}


def _get(name, builder):
    if name not in _NC_CACHE:
        _NC_CACHE[name] = builder()
    return _NC_CACHE[name]


def _in_map(hp, c, extra):
    im = {"xpi": hp["x_cores"][c], "xpre": hp["xpre_cores"][c],
          "wqkv": hp["wqkv"], "bqkv": hp["bqkv"], "dmaskT": hp["dmaskT"],
          "dqb": hp["dqb"], "dk": hp["dk"],
          "pk": hp["pk_odd"] if c % 2 else hp["pk_even"],
          "ident": hp["ident"], "wout": hp["wout"], "bout": hp["bout"],
          "p1": hp["p1"], "p1b": hp["p1b"], "p2g": hp["p2g"],
          "p2b": hp["p2b"], "wq": hp["wq"], "kvkT": hp["kvkT"],
          "kvv": hp["kvv"], "wo": hp["wo"], "gupk": hp["gupk"],
          "gusc": hp["gusc"], "dnpk": hp["dnpk"], "dnsc": hp["dnsc"]}
    im.update(extra)
    return im


def kernel(**inputs):
    hp = _host_prep(inputs)
    cores = list(range(NCORE))
    nc = _get("fused", _build_fused)
    im = [_in_map(hp, c, {}) for c in cores]
    res = run_bass_kernel_spmd(nc, im, core_ids=cores)
    out = np.empty((4, 2048, DM), np.float32)
    for c in cores:
        b, half = c // 2, c % 2
        out[b, half * TOK:(half + 1) * TOK] = _unfold_out(res.results[c]["xout"])
    return out
